# version 03
# baseline (speedup 1.0000x reference)
import numpy as np
import concourse.bass as bass
import concourse.mybir as mybir
from concourse.bass_utils import run_bass_kernel_spmd

F32 = mybir.dt.float32
BF16 = mybir.dt.bfloat16
AF = mybir.ActivationFunctionType
ALU = mybir.AluOpType

NCORES = 8
B, T, D, S = 16, 2048, 1024, 64
BL = B // NCORES
TOK = BL * T
C = 512
NCH = TOK // C
CPS = T // C
DC = D // 128
EPS = 1e-5


class Sem:
    def __init__(self, h):
        self.h = h
        self.n = 0


class Stream:
    def __init__(self, name):
        self.name = name
        self.ops = []
        self.hw = {}

    def emit(self, fn):
        self.ops.append(fn)

    def wait(self, sem, val):
        if not val or val <= self.hw.get(id(sem), 0):
            return
        self.hw[id(sem)] = val
        self.ops.append(lambda e, s=sem.h, v=val: e.wait_ge(s, v))

    def inc(self, sem, k, fn):
        sem.n += k
        self.ops.append(lambda e, f=fn, s=sem.h, kk=k: f(e).then_inc(s, kk))
        return sem.n

    def replay(self, e):
        for f in self.ops:
            f(e)


class Bank:
    def __init__(self, idx):
        self.idx = idx
        self.rel = []

    def acquire(self, stream):
        for sem, val in self.rel:
            stream.wait(sem, val)
        self.rel = []

    def release(self, sem, val):
        self.rel.append((sem, val))


def build_program():
    nc = bass.Bass(trn_type="TRN2", target_bir_lowering=False)

    x_d = nc.dram_tensor("x", [D, TOK], F32, kind="ExternalInput").ap()
    y_d = nc.dram_tensor("y", [D, TOK], F32, kind="ExternalOutput").ap()
    inw_d = nc.dram_tensor("in_wT", [D, 2 * D], F32, kind="ExternalInput").ap()
    gw_d = nc.dram_tensor("g_wT", [D, 3 * S], F32, kind="ExternalInput").ap()
    ow_d = nc.dram_tensor("out_wT", [D + S, D], F32, kind="ExternalInput").ap()
    fw_d = nc.dram_tensor("ffn_wT", [D, D], F32, kind="ExternalInput").ap()
    vr_d = nc.dram_tensor("v_r", [D, 32], F32, kind="ExternalInput").ap()
    ct_d = nc.dram_tensor("c_T", [32, D], F32, kind="ExternalInput").ap()
    eye_d = nc.dram_tensor("eye2", [128, 256], F32, kind="ExternalInput").ap()
    inb_d = nc.dram_tensor("in_bv_hbm", [128, 16], F32, kind="ExternalInput").ap()
    nw_d = nc.dram_tensor("norm_wv_hbm", [128, DC], F32, kind="ExternalInput").ap()
    nb_d = nc.dram_tensor("norm_bv_hbm", [128, DC], F32, kind="ExternalInput").ap()
    ob_d = nc.dram_tensor("out_bv_hbm", [128, DC], F32, kind="ExternalInput").ap()
    fb_d = nc.dram_tensor("ffn_bv_hbm", [128, DC], F32, kind="ExternalInput").ap()
    gb_d = nc.dram_tensor("g_bv_hbm", [S, 3], F32, kind="ExternalInput").ap()
    cw_d = nc.dram_tensor("conv_wv_hbm", [128, 3 * DC], F32, kind="ExternalInput").ap()
    cb_d = nc.dram_tensor("conv_bv_hbm", [128, DC], F32, kind="ExternalInput").ap()
    dv_d = nc.dram_tensor("d_v_hbm", [128, DC], F32, kind="ExternalInput").ap()
    al_d = nc.dram_tensor("alog_v_hbm", [S, 1], F32, kind="ExternalInput").ap()

    x_r = x_d.rearrange("(c p) t -> c p t", p=128)
    y_r = y_d.rearrange("(c p) t -> c p t", p=128)

    A = nc.alloc_sbuf_tensor
    in_wb = A("in_wb", [128, DC * 2 * D], BF16)
    g_wb = A("g_wb", [128, DC * 3 * S], BF16)
    out_wb = A("out_wb", [128, DC * D], BF16)
    out_wyb = A("out_wyb", [S, D], BF16)
    ffn_wb = A("ffn_wb", [128, DC * D], BF16)
    vrb = A("vrb", [128, DC * 32], BF16)
    ctb = A("ctb", [32, D], BF16)
    eyeb = A("eyeb", [128, 256], BF16)
    onesn = A("onesn", [128, 128], BF16)
    onesp = A("onesp", [128, 128], BF16)

    in_bv = A("in_bv", [128, 16], F32)
    norm_wv = A("norm_wv", [128, DC], F32)
    norm_bv = A("norm_bv", [128, DC], F32)
    out_bv = A("out_bv", [128, DC], F32)
    ffn_bv = A("ffn_bv", [128, DC], F32)
    g_bv = A("g_bv", [S, 3], F32)
    conv_wv = A("conv_wv", [128, 3 * DC], F32)
    conv_bv = A("conv_bv", [128, DC], F32)
    d_v = A("d_v", [128, DC], F32)
    alog_v = A("alog_v", [S, 1], F32)
    tmpa = A("tmpa", [S, 1], F32)
    aT = A("aT", [S, 1], F32)
    a1e4T = A("a1e4T", [S, 1], F32)
    epsT = A("epsT", [128, 1], F32)

    x0b = A("x0b", [128, 2 * DC * C], BF16)
    sqt = A("sqt", [128, DC * C], BF16)
    tt = A("tt", [128, DC * C], BF16)
    nbb = A("nbb", [128, DC], BF16)
    in_bv2 = A("in_bv2", [128, 16], F32)
    xp = A("xp", [128, 2 * DC * 514], BF16)
    zt = A("zt", [128, 2 * DC * C], BF16)
    xcr = A("xcr", [128, 2 * C], BF16)
    xconv = A("xconv", [128, DC * C], BF16)
    acc = A("acc", [128, 3 * C], F32)
    gtA = A("gtA", [S, C], F32)
    gtB = A("gtB", [S, C], F32)
    gtC = A("gtC", [S, C], F32)
    bT = A("bT", [S, C], F32)
    cTt = A("cTt", [S, C], F32)
    states = A("states", [S, C], F32)
    carry = A("carry", [S, 1], F32)
    yB = A("yB", [S, C], BF16)
    uv = A("uv", [128, DC * C], F32)
    ub = A("ub", [128, DC * C], BF16)
    x1b = A("x1b", [128, DC * C], BF16)
    bsr_b = A("bsr_b", [32, C], BF16)
    sq_mu = A("sq_mu", [128, C], F32)
    var_b = A("var_b", [128, C], F32)
    lnv = A("lnv", [128, C], F32)
    rs_b = A("rs_b", [128, C], F32)

    ps = [nc.alloc_psum_tensor(f"ps{i}", [128, C], F32) for i in range(8)]
    bank = [Bank(i) for i in range(8)]

    def inw_t(c, m):
        return in_wb.ap()[:, c * 2 * D + m * 128: c * 2 * D + (m + 1) * 128]

    def gw_t(c, g):
        return g_wb.ap()[:, c * 3 * S + g * S: c * 3 * S + (g + 1) * S]

    def ow_t(c, m):
        return out_wb.ap()[:, c * D + m * 128: c * D + (m + 1) * 128]

    def fw_t(c, m):
        return ffn_wb.ap()[:, c * D + m * 128: c * D + (m + 1) * 128]

    def x0_t(j, c):
        return x0b.ap()[:, (j * DC + c) * C: (j * DC + c + 1) * C]

    def ch(tile, c):
        return tile.ap()[:, c * C: (c + 1) * C]

    def xp_t(j, c, lo, hi):
        base = (j * DC + c) * 514
        return xp.ap()[:, base + lo: base + hi]

    def z_t(j, c):
        return zt.ap()[:, (j * DC + c) * C: (j * DC + c + 1) * C]

    def acc_t(c):
        return acc.ap()[:, (c % 2) * C:(c % 2 + 1) * C]

    def accb_t(c):
        return acc.ap()[:, 2 * C:3 * C]

    def xcr_t(c):
        return xcr.ap()[:, (c % 2) * C:(c % 2 + 1) * C]

    with (
        nc.semaphore("dma_w") as h_dw,
        nc.semaphore("dma_g") as h_dg,
        nc.semaphore("dma_o") as h_do,
        nc.semaphore("s_se") as h_se,
        nc.semaphore("s_ve") as h_ve,
        nc.semaphore("s_te") as h_te,
        nc.Block() as block,
    ):
        dw, dg, do = Sem(h_dw), Sem(h_dg), Sem(h_do)
        se, ve, te = Sem(h_se), Sem(h_ve), Sem(h_te)
        Ssy, Sgp, Sse, Sve, Ste = (Stream("sync"), Stream("gpsimd"),
                                   Stream("scalar"), Stream("vector"),
                                   Stream("tensor"))

        a2_done = [0] * NCH
        v1a_done = [0] * NCH
        rsA_done = [0] * NCH
        v2_done = [[0] * DC for _ in range(NCH)]
        t_done = [[0] * DC for _ in range(NCH)]
        h_done = [[0] * 16 for _ in range(NCH)]
        hevac = [[0] * 16 for _ in range(NCH)]
        edge_r = [[0] * DC for _ in range(NCH)]
        gproj_done = [0] * NCH
        conv_done = [[0] * DC for _ in range(NCH)]
        prev_tanh = [0] * NCH
        xcd_done = [[0] * DC for _ in range(NCH)]
        op_done = [[0] * DC for _ in range(NCH)]
        uev = [[0] * DC for _ in range(NCH)]
        ln2s_done = [0] * NCH
        v1b_done = [0] * NCH
        rs2_done = [0] * NCH
        x1_done = [[0] * DC for _ in range(NCH)]
        ffn_last = [0] * NCH
        vev = [[0] * DC for _ in range(NCH)]
        ln3s_done = [0] * NCH
        v1c_done = [0] * NCH
        rs3_done = [0] * NCH
        out_done = [0] * NCH
        x0_ld = [0] * NCH

        for dram, sb in [(inb_d, in_bv), (nw_d, norm_wv), (nb_d, norm_bv),
                         (ob_d, out_bv), (fb_d, ffn_bv), (gb_d, g_bv),
                         (cw_d, conv_wv), (cb_d, conv_bv), (dv_d, d_v),
                         (al_d, alog_v)]:
            Ssy.inc(dw, 16, lambda e, d_=dram, s_=sb: e.dma_start(out=s_.ap(), in_=d_))
        n_vec_loads = dw.n

        Sgp.inc(dg, 16, lambda e: e.dma_start(out=eyeb.ap(), in_=eye_d))
        inw_ld = Sgp.inc(dg, 16, lambda e: e.dma_start(
            out=in_wb.ap().rearrange("p (c m) -> p c m", c=DC),
            in_=inw_d.rearrange("(c p) m -> p c m", p=128)))
        gw_ld = Sgp.inc(dg, 16, lambda e: e.dma_start(
            out=g_wb.ap().rearrange("p (c m) -> p c m", c=DC),
            in_=gw_d.rearrange("(c p) m -> p c m", p=128)))
        Sgp.inc(dg, 16, lambda e: e.dma_start(
            out=out_wb.ap().rearrange("p (c m) -> p c m", c=DC),
            in_=ow_d[0:D, :].rearrange("(c p) m -> p c m", p=128)))
        ow_ld = Sgp.inc(dg, 16, lambda e: e.dma_start(
            out=out_wyb.ap(), in_=ow_d[D:D + S, :]))
        Sgp.inc(dg, 16, lambda e: e.dma_start(
            out=ffn_wb.ap().rearrange("p (c m) -> p c m", c=DC),
            in_=fw_d.rearrange("(c p) m -> p c m", p=128)))
        Sgp.inc(dg, 16, lambda e: e.dma_start(
            out=vrb.ap().rearrange("p (c m) -> p c m", c=DC),
            in_=vr_d.rearrange("(c p) m -> p c m", p=128)))
        ffnw_ld = Sgp.inc(dg, 16, lambda e: e.dma_start(out=ctb.ap(), in_=ct_d))

        def emit_x0_load(i):
            t0 = i * C
            x0_ld[i] = Sgp.inc(dg, 16, lambda e, j=i % 2, t0_=t0: e.dma_start(
                out=x0b.ap()[:, j * DC * C:(j + 1) * DC * C]
                    .rearrange("p (c t) -> p c t", c=DC),
                in_=x_d.rearrange("(c p) t -> p c t", p=128)[:, :, t0_: t0_ + C]))

        emit_x0_load(0)
        emit_x0_load(1)

        Sve.emit(lambda e: e.memset(onesn.ap(), -1.0 / D))
        Sve.emit(lambda e: e.memset(onesp.ap(), 1.0 / D))
        memset_done = Sve.inc(ve, 1, lambda e: e.memset(epsT.ap(), EPS))
        Sse.wait(dw, n_vec_loads)
        a_exp = Sse.inc(se, 1, lambda e: e.activation(
            out=tmpa.ap(), in_=alog_v.ap(), func=AF.Exp))
        Sve.wait(se, a_exp)
        Sve.emit(lambda e: e.tensor_scalar_mul(aT.ap(), tmpa.ap(), -1.0))
        a_prep = Sve.inc(ve, 1, lambda e: e.tensor_scalar_mul(
            a1e4T.ap(), aT.ap(), 1e-4))
        nbb_done = Sse.inc(se, 1, lambda e: e.activation(
            out=nbb.ap(), in_=norm_bv.ap(), func=AF.Copy))
        Ste.wait(dg, inw_ld)
        Ste.wait(se, nbb_done)
        mv_evac = [0, 0]
        for m in range(16):
            Ste.wait(se, mv_evac[m % 2])
            for c in range(DC):
                fn = lambda e, c_=c, m_=m: e.matmul(
                    ps[m_ % 2].ap()[:, 0:1], inw_t(c_, m_),
                    nbb.ap()[:, c_:c_ + 1],
                    start=(c_ == 0), stop=(c_ == DC - 1))
                if c == DC - 1:
                    mv = Ste.inc(te, 1, fn)
                else:
                    Ste.emit(fn)
            Sse.wait(te, mv)
            mv_evac[m % 2] = Sse.inc(se, 1, lambda e, m_=m: e.activation(
                out=in_bv2.ap()[:, m_:m_ + 1], in_=ps[m_ % 2].ap()[:, 0:1],
                func=AF.Identity, bias=in_bv.ap()[:, m_:m_ + 1]))
        bank[0].release(se, mv_evac[0])
        bank[1].release(se, mv_evac[1])

        sq_ready = [0] * NCH

        def emit_s1(i):
            j = i % 2
            Sse.wait(dg, x0_ld[i])
            if i >= 2:
                Sse.wait(te, ln3s_done[i - 2])
            for c in range(DC):
                fn = lambda e, c_=c: e.activation(
                    out=ch(sqt, c_), in_=x0_t(j, c_), func=AF.Square)
                if c == DC - 1:
                    sq_ready[i] = Sse.inc(se, 1, fn)
                else:
                    Sse.emit(fn)

        def stage_a_stats1(i):
            j = i % 2
            if not sq_ready[i]:
                emit_s1(i)
            Ste.wait(dg, x0_ld[i])
            Ste.wait(ve, memset_done)
            bank[4].acquire(Ste)
            for c in range(DC):
                fn = lambda e, c_=c: e.matmul(
                    ps[4].ap(), onesn.ap(), x0_t(j, c_),
                    start=(c_ == 0), stop=(c_ == DC - 1))
                if c == DC - 1:
                    a1_done[i] = Ste.inc(te, 1, fn)
                else:
                    Ste.emit(fn)

        def stage_a_stats2(i):
            li = i % CPS
            j = i % 2
            Ste.wait(se, sq_ready[i])
            bank[5].acquire(Ste)
            for c in range(DC):
                fn = lambda e, c_=c: e.matmul(
                    ps[5].ap(), onesp.ap(), ch(sqt, c_),
                    start=(c_ == 0), stop=(c_ == DC - 1))
                if c == DC - 1:
                    a2_done[i] = Ste.inc(te, 1, fn)
                else:
                    Ste.emit(fn)
            Sse.wait(te, a1_done[i])
            if i >= 2:
                Sse.wait(ve, v1c_done[i - 2])
            s2 = Sse.inc(se, 1, lambda e: e.activation(
                out=sq_mu.ap(), in_=ps[4].ap(), func=AF.Square))
            Sve.wait(te, a2_done[i])
            Sve.wait(se, s2)
            if i >= 2:
                Sve.wait(se, rs3_done[i - 2])
            v1a_done[i] = Sve.inc(ve, 1, lambda e: e.tensor_tensor(
                var_b.ap(), ps[5].ap(), sq_mu.ap(), ALU.subtract))
            Sse.wait(ve, v1a_done[i])
            Sse.emit(lambda e: e.activation(out=lnv.ap(), in_=var_b.ap(),
                                            func=AF.Ln, bias=epsT.ap()))
            if i >= 2:
                Sse.wait(ve, out_done[i - 2])
            rsA_done[i] = Sse.inc(se, 1, lambda e: e.activation(
                out=rs_b.ap(), in_=lnv.ap(), func=AF.Exp, scale=-0.5))
            Sve.wait(se, rsA_done[i])
            if i >= 1:
                Sve.wait(te, h_done[i - 1][15])
            for c in range(DC):
                Sve.emit(lambda e, c_=c: e.tensor_tensor(
                    ps[5].ap(), x0_t(j, c_), ps[4].ap(), ALU.add))
                v2_done[i][c] = Sve.inc(ve, 1, lambda e, c_=c: e.scalar_tensor_tensor(
                    ch(tt, c_), ps[5].ap(), norm_wv.ap()[:, c_:c_ + 1],
                    rs_b.ap(), ALU.mult, ALU.mult))
                t_done[i][c] = v2_done[i][c]
            bank[4].release(ve, v2_done[i][DC - 1])
            bank[5].release(ve, v2_done[i][DC - 1])
            if i >= 2:
                Sse.wait(ve, conv_done[i - 2][DC - 1])
            if li == 0:
                for m in range(DC):
                    Sse.emit(lambda e, m_=m: e.activation(
                        out=xp_t(j, m_, 0, 1), in_=epsT.ap(),
                        func=AF.Identity, scale=0.0))
            else:
                for m in range(DC):
                    Sse.emit(lambda e, m_=m: e.activation(
                        out=xp_t(j, m_, 0, 1),
                        in_=xp_t(1 - j, m_, 512, 513), func=AF.Identity))

        def stage_a_h(i, m_lo, m_hi):
            li = i % CPS
            j = i % 2
            Ste.wait(dg, inw_ld)
            for m in range(m_lo, m_hi):
                bk = bank[m % 2]
                bk.acquire(Ste)
                for c in range(DC):
                    Ste.wait(ve, t_done[i][c])
                    fn = lambda e, c_=c, m_=m: e.matmul(
                        ps[m_ % 2].ap(), inw_t(c_, m_), ch(tt, c_),
                        start=(c_ == 0), stop=(c_ == DC - 1))
                    if c == DC - 1:
                        h_done[i][m] = Ste.inc(te, 1, fn)
                    else:
                        Ste.emit(fn)
                Sse.wait(te, h_done[i][m])
                if m < DC:
                    if i >= 1 and li != 0:
                        Sse.emit(lambda e, m_=m: e.activation(
                            out=xp_t(j, m_, 1, 513), in_=ps[m_ % 2].ap(),
                            func=AF.Identity, bias=in_bv2.ap()[:, m_:m_ + 1]))
                        hevac[i][m] = Sse.inc(se, 1, lambda e, m_=m: e.activation(
                            out=xp_t(1 - j, m_, 513, 514),
                            in_=ps[m_ % 2].ap()[:, 0:1],
                            func=AF.Identity, bias=in_bv2.ap()[:, m_:m_ + 1]))
                        edge_r[i - 1][m] = hevac[i][m]
                    else:
                        hevac[i][m] = Sse.inc(se, 1, lambda e, m_=m: e.activation(
                            out=xp_t(j, m_, 1, 513), in_=ps[m_ % 2].ap(),
                            func=AF.Identity, bias=in_bv2.ap()[:, m_:m_ + 1]))
                else:
                    if i >= 2:
                        Sse.wait(te, gproj_done[i - 2])
                    hevac[i][m] = Sse.inc(se, 1, lambda e, m_=m: e.activation(
                        out=z_t(j, m_ - DC), in_=ps[m_ % 2].ap(),
                        func=AF.Identity, bias=in_bv2.ap()[:, m_:m_ + 1]))
                bk.release(se, hevac[i][m])

        def stage_b_g(b):
            li = b % CPS
            j = b % 2
            if li == CPS - 1:
                ze = 0
                for m in range(DC):
                    fn = lambda e, m_=m: e.activation(
                        out=xp_t(j, m_, 513, 514), in_=epsT.ap(),
                        func=AF.Identity, scale=0.0)
                    if m == DC - 1:
                        ze = Sse.inc(se, 1, fn)
                    else:
                        Sse.emit(fn)
                for m in range(DC):
                    edge_r[b][m] = ze
            Ste.wait(dg, gw_ld)
            Ste.wait(se, hevac[b][15])
            gd = 0
            for g, bki in [(0, 2), (1, 3), (2, 6)]:
                bank[bki].acquire(Ste)
                for c in range(DC):
                    fn = lambda e, c_=c, g_=g, k_=bki: e.matmul(
                        ps[k_].ap()[0:S, :], gw_t(c_, g_), z_t(j, c_),
                        start=(c_ == 0), stop=(c_ == DC - 1))
                    if c == DC - 1:
                        gd = Ste.inc(te, 1, fn)
                    else:
                        Ste.emit(fn)
            gproj_done[b] = gd

        def stage_b_mid(b):
            li = b % CPS
            j = b % 2
            tanh_done = [0] * DC
            for c in range(DC):
                Sve.wait(se, edge_r[b][c])
                Sve.wait(se, hevac[b][c])
                if c >= 2:
                    Sve.wait(se, tanh_done[c - 2])
                elif b >= 1:
                    Sve.wait(se, prev_tanh[b - 1])
                if c == 0 and b >= 1:
                    Sve.wait(te, op_done[b - 1][DC - 1])
                Sve.emit(lambda e, c_=c: e.tensor_scalar_mul(
                    acc_t(c_), xp_t(j, c_, 2, 514),
                    conv_wv.ap()[:, c_ * 3 + 2:c_ * 3 + 3]))
                Sve.emit(lambda e, c_=c: e.scalar_tensor_tensor(
                    accb_t(c_), xp_t(j, c_, 1, 513),
                    conv_wv.ap()[:, c_ * 3 + 1:c_ * 3 + 2],
                    acc_t(c_), ALU.mult, ALU.add))
                conv_done[b][c] = Sve.inc(ve, 1, lambda e, c_=c: e.scalar_tensor_tensor(
                    acc_t(c_), xp_t(j, c_, 0, 512),
                    conv_wv.ap()[:, c_ * 3:c_ * 3 + 1],
                    accb_t(c_), ALU.mult, ALU.add))
                Sse.wait(ve, conv_done[b][c])
                tanh_done[c] = Sse.inc(se, 1, lambda e, c_=c: e.activation(
                    out=xcr_t(c_), in_=acc_t(c_),
                    func=AF.Tanh, bias=conv_bv.ap()[:, c_:c_ + 1]))
                Sve.wait(se, tanh_done[c])
                xcd_done[b][c] = Sve.inc(ve, 1, lambda e, c_=c: e.tensor_scalar_mul(
                    ch(xconv, c_), xcr_t(c_), d_v.ap()[:, c_:c_ + 1]))
            prev_tanh[b] = tanh_done[DC - 1]

            Sse.wait(te, gproj_done[b])
            bt_done = Sse.inc(se, 1, lambda e: e.activation(
                out=bT.ap(), in_=ps[3].ap()[0:S, :], func=AF.Tanh,
                bias=g_bv.ap()[:, 1:2]))
            bank[3].release(se, bt_done)
            ct_done = Sse.inc(se, 1, lambda e: e.activation(
                out=cTt.ap(), in_=ps[6].ap()[0:S, :], func=AF.Tanh,
                bias=g_bv.ap()[:, 2:3]))
            bank[6].release(se, ct_done)
            Sse.emit(lambda e: e.activation(
                out=gtA.ap(), in_=ps[2].ap()[0:S, :], func=AF.Abs,
                bias=g_bv.ap()[:, 0:1]))
            g_rel = Sse.inc(se, 1, lambda e: e.activation(
                out=gtC.ap(), in_=ps[2].ap()[0:S, :], func=AF.Relu,
                bias=g_bv.ap()[:, 0:1]))
            bank[2].release(se, g_rel)
            Sse.emit(lambda e: e.activation(
                out=gtB.ap(), in_=gtA.ap(), func=AF.Exp, scale=-1.0))
            gln = Sse.inc(se, 1, lambda e: e.activation(
                out=gtA.ap(), in_=gtB.ap(), func=AF.Ln, bias=1.0))
            Sve.wait(se, gln)
            spt = Sve.inc(ve, 1, lambda e: e.tensor_tensor(
                gtB.ap(), gtC.ap(), gtA.ap(), ALU.add))
            Sse.wait(ve, spt)
            Sse.wait(ve, a_prep)
            dec_done = Sse.inc(se, 1, lambda e: e.activation(
                out=gtC.ap(), in_=gtB.ap(), func=AF.Exp,
                scale=aT.ap(), bias=a1e4T.ap()))

            Sve.wait(se, dec_done)
            ini = 0.0 if li == 0 else carry.ap()
            Sve.emit(lambda e, ini_=ini: e.tensor_tensor_scan(
                out=states.ap(), data0=gtC.ap(), data1=bT.ap(),
                initial=ini_, op0=ALU.mult, op1=ALU.add))
            Sve.emit(lambda e: e.tensor_copy(carry.ap(), states.ap()[:, C - 1:C]))
            y_done = Sve.inc(ve, 1, lambda e: e.tensor_tensor(
                yB.ap(), cTt.ap(), states.ap(), ALU.mult))

            Ste.wait(dg, ow_ld)
            for m in range(DC):
                bk = bank[2 + m % 2]
                bk.acquire(Ste)
                for c in range(DC):
                    Ste.wait(ve, xcd_done[b][c])
                    Ste.emit(lambda e, c_=c, m_=m: e.matmul(
                        ps[2 + m_ % 2].ap(), ow_t(c_, m_), ch(xconv, c_),
                        start=(c_ == 0), stop=False))
                Ste.wait(ve, y_done)
                Ste.emit(lambda e, m_=m: e.matmul(
                    ps[2 + m_ % 2].ap(), out_wyb.ap()[:, m_ * 128:(m_ + 1) * 128],
                    yB.ap(), start=False, stop=False))
                op_done[b][m] = Ste.inc(te, 1, lambda e, m_=m: e.matmul(
                    ps[2 + m_ % 2].ap(), eyeb.ap()[:, 0:128], x0_t(j, m_),
                    start=False, stop=True))
                Sse.wait(te, op_done[b][m])
                if m == 0:
                    if b >= 1:
                        Sse.wait(do, 16 * b)
                        Sse.wait(te, ln3s_done[b - 1])
                    Sse.wait(te, a2_done[b + 1] if b + 1 < NCH else a2_done[b])
                Sse.emit(lambda e, m_=m: e.activation(
                    out=ch(uv, m_), in_=ps[2 + m_ % 2].ap(), func=AF.Identity,
                    bias=out_bv.ap()[:, m_:m_ + 1]))
                Sse.emit(lambda e, m_=m: e.activation(
                    out=ch(sqt, m_), in_=ps[2 + m_ % 2].ap(), func=AF.Square,
                    bias=out_bv.ap()[:, m_:m_ + 1]))
                uev[b][m] = Sse.inc(se, 1, lambda e, m_=m: e.activation(
                    out=ch(ub, m_), in_=ps[2 + m_ % 2].ap(), func=AF.Identity,
                    bias=out_bv.ap()[:, m_:m_ + 1]))
                bk.release(se, uev[b][m])

            bank[6].acquire(Ste)
            for c in range(DC):
                Ste.wait(se, uev[b][c])
                fn = lambda e, c_=c: e.matmul(
                    ps[6].ap(), onesn.ap(), ch(ub, c_),
                    start=(c_ == 0), stop=(c_ == DC - 1))
                if c == DC - 1:
                    mu2_done[b] = Ste.inc(te, 1, fn)
                else:
                    Ste.emit(fn)
            bank[7].acquire(Ste)
            for c in range(DC):
                fn = lambda e, c_=c: e.matmul(
                    ps[7].ap(), onesp.ap(), ch(sqt, c_),
                    start=(c_ == 0), stop=(c_ == DC - 1))
                if c == DC - 1:
                    ln2s_done[b] = Ste.inc(te, 1, fn)
                else:
                    Ste.emit(fn)

        def stage_b_tail(b):
            li = b % CPS
            j = b % 2
            Sse.wait(te, mu2_done[b])
            if b + 1 < NCH:
                Sse.wait(ve, v1a_done[b + 1])
            sm2 = Sse.inc(se, 1, lambda e: e.activation(
                out=sq_mu.ap(), in_=ps[6].ap(), func=AF.Square))
            Sve.wait(te, ln2s_done[b])
            Sve.wait(se, sm2)
            if b + 1 < NCH:
                Sve.wait(se, rsA_done[b + 1])
            v1b_done[b] = Sve.inc(ve, 1, lambda e: e.tensor_tensor(
                var_b.ap(), ps[7].ap(), sq_mu.ap(), ALU.subtract))
            Sse.wait(ve, v1b_done[b])
            Sse.emit(lambda e: e.activation(out=lnv.ap(), in_=var_b.ap(),
                                            func=AF.Ln, bias=epsT.ap()))
            if b + 1 < NCH:
                Sse.wait(ve, v2_done[b + 1][DC - 1])
            rs2_done[b] = Sse.inc(se, 1, lambda e: e.activation(
                out=rs_b.ap(), in_=lnv.ap(), func=AF.Exp, scale=-0.5))
            Sve.wait(se, rs2_done[b])
            if b >= 1:
                Sve.wait(te, ffn_last[b - 1])
            for c in range(DC):
                Sve.emit(lambda e, c_=c: e.tensor_tensor(
                    ps[7].ap(), ch(uv, c_), ps[6].ap(), ALU.add))
                x1_done[b][c] = Sve.inc(ve, 1, lambda e, c_=c: e.tensor_tensor(
                    ch(x1b, c_), ps[7].ap(), rs_b.ap(), ALU.mult))
            bank[6].release(ve, x1_done[b][DC - 1])
            bank[7].release(ve, x1_done[b][DC - 1])

            Ste.wait(dg, ffnw_ld)
            bank[6].acquire(Ste)
            bsr_done = 0
            for c in range(DC):
                Ste.wait(ve, x1_done[b][c])
                fn = lambda e, c_=c: e.matmul(
                    ps[6].ap()[0:32, :], vrb.ap()[:, c_ * 32:(c_ + 1) * 32],
                    ch(x1b, c_), start=(c_ == 0), stop=(c_ == DC - 1))
                if c == DC - 1:
                    bsr_done = Ste.inc(te, 1, fn)
                else:
                    Ste.emit(fn)
            Sse.wait(te, bsr_done)
            s12 = Sse.inc(se, 1, lambda e: e.activation(
                out=bsr_b.ap(), in_=ps[6].ap()[0:32, :], func=AF.Identity))
            bank[6].release(se, s12)
            for m in range(DC):
                bk = bank[2 + m % 2]
                bk.acquire(Ste)
                for c in range(DC):
                    Ste.emit(lambda e, c_=c, m_=m: e.matmul(
                        ps[2 + m_ % 2].ap(), fw_t(c_, m_), ch(x1b, c_),
                        start=(c_ == 0), stop=False))
                Ste.wait(se, s12)
                Ste.emit(lambda e, m_=m: e.matmul(
                    ps[2 + m_ % 2].ap(), ctb.ap()[:, m_ * 128:(m_ + 1) * 128],
                    bsr_b.ap(), start=False, stop=False))
                f_done = Ste.inc(te, 1, lambda e, m_=m: e.matmul(
                    ps[2 + m_ % 2].ap(), eyeb.ap()[:, 128:256], ch(x1b, m_),
                    start=False, stop=True))
                if m == DC - 1:
                    ffn_last[b] = f_done
                Sse.wait(te, f_done)
                Sse.wait(te, ln2s_done[b])
                Sse.wait(ve, x1_done[b][m])
                Sse.emit(lambda e, m_=m: e.activation(
                    out=ch(uv, m_), in_=ps[2 + m_ % 2].ap(), func=AF.Identity,
                    bias=ffn_bv.ap()[:, m_:m_ + 1]))
                Sse.emit(lambda e, m_=m: e.activation(
                    out=ch(sqt, m_), in_=ps[2 + m_ % 2].ap(), func=AF.Square,
                    bias=ffn_bv.ap()[:, m_:m_ + 1]))
                vev[b][m] = Sse.inc(se, 1, lambda e, m_=m: e.activation(
                    out=ch(ub, m_), in_=ps[2 + m_ % 2].ap(), func=AF.Identity,
                    bias=ffn_bv.ap()[:, m_:m_ + 1]))
                bk.release(se, vev[b][m])

            bank[6].acquire(Ste)
            for c in range(DC):
                Ste.wait(se, vev[b][c])
                fn = lambda e, c_=c: e.matmul(
                    ps[6].ap(), onesn.ap(), ch(ub, c_),
                    start=(c_ == 0), stop=(c_ == DC - 1))
                if c == DC - 1:
                    mu3 = Ste.inc(te, 1, fn)
                else:
                    Ste.emit(fn)
            bank[7].acquire(Ste)
            for c in range(DC):
                fn = lambda e, c_=c: e.matmul(
                    ps[7].ap(), onesp.ap(), ch(sqt, c_),
                    start=(c_ == 0), stop=(c_ == DC - 1))
                if c == DC - 1:
                    ln3s_done[b] = Ste.inc(te, 1, fn)
                else:
                    Ste.emit(fn)
            Sse.wait(te, mu3)
            Sse.wait(ve, v1b_done[b])
            sm3 = Sse.inc(se, 1, lambda e: e.activation(
                out=sq_mu.ap(), in_=ps[6].ap(), func=AF.Square))
            Sve.wait(te, ln3s_done[b])
            Sve.wait(se, sm3)
            Sve.wait(se, rs2_done[b])
            v1c_done[b] = Sve.inc(ve, 1, lambda e: e.tensor_tensor(
                var_b.ap(), ps[7].ap(), sq_mu.ap(), ALU.subtract))
            Sse.wait(ve, v1c_done[b])
            Sse.emit(lambda e: e.activation(out=lnv.ap(), in_=var_b.ap(),
                                            func=AF.Ln, bias=epsT.ap()))
            Sse.wait(ve, x1_done[b][DC - 1])
            rs3_done[b] = Sse.inc(se, 1, lambda e: e.activation(
                out=rs_b.ap(), in_=lnv.ap(), func=AF.Exp, scale=-0.5))
            Sve.wait(se, rs3_done[b])
            for c in range(DC):
                Sve.emit(lambda e, c_=c: e.tensor_tensor(
                    ps[7].ap(), ch(uv, c_), ps[6].ap(), ALU.add))
                fn = lambda e, c_=c: e.tensor_tensor(
                    ch(uv, c_), ps[7].ap(), rs_b.ap(), ALU.mult)
                if c == DC - 1:
                    out_done[b] = Sve.inc(ve, 1, fn)
                else:
                    Sve.emit(fn)
            bank[6].release(ve, out_done[b])
            bank[7].release(ve, out_done[b])

            Ssy.wait(ve, out_done[b])
            t0 = b * C
            Ssy.inc(do, 16, lambda e, t0_=t0: e.dma_start(
                out=y_d.rearrange("(c p) t -> p c t", p=128)[:, :, t0_:t0_ + C],
                in_=uv.ap().rearrange("p (c t) -> p c t", c=DC)))

            if b + 2 < NCH:
                Sgp.wait(te, op_done[b][DC - 1])
                emit_x0_load(b + 2)
                emit_s1(b + 2)

        a1_done = [0] * NCH
        mu2_done = [0] * NCH
        for i in range(NCH + 1):
            b = i - 1
            if i < NCH:
                stage_a_stats1(i)
            if b >= 0:
                stage_b_g(b)
            if i < NCH:
                stage_a_stats2(i)
                stage_a_h(i, 0, DC)
            if b >= 0:
                stage_b_mid(b)
            if i < NCH:
                stage_a_h(i, DC, 16)
            if b >= 0:
                stage_b_tail(b)

        final = do.n
        Ssy.emit(lambda e, v=final: e.wait_ge(do.h, v))

        block.sync(Ssy.replay)
        block.gpsimd(Sgp.replay)
        block.scalar(Sse.replay)
        block.vector(Sve.replay)
        block.tensor(Ste.replay)

    return nc


_NC = None


def kernel(L0_attn_x, L0_attn_norm_w, L0_attn_norm_b, L0_attn_in_w, L0_attn_in_b,
           L0_attn_conv_w, L0_attn_conv_b, L0_attn_dt_w, L0_attn_dt_b,
           L0_attn_b_w, L0_attn_b_b, L0_attn_c_w, L0_attn_c_b, L0_attn_a_log,
           L0_attn_d, L0_attn_out_w, L0_attn_out_b, L0_ffn_W, L0_ffn_b,
           L0_ffn_V_r, L0_ffn_C):
    global _NC
    if _NC is None:
        _NC = build_program()
    nc = _NC

    f32 = np.float32
    asc = np.ascontiguousarray
    x = np.asarray(L0_attn_x, dtype=f32)

    shared = {
        "in_wT": asc(np.asarray(L0_attn_in_w, f32).T),
        "g_wT": asc(np.concatenate([np.asarray(L0_attn_dt_w, f32),
                                    np.asarray(L0_attn_b_w, f32),
                                    np.asarray(L0_attn_c_w, f32)], 0).T),
        "out_wT": asc(np.asarray(L0_attn_out_w, f32).T),
        "ffn_wT": asc(np.asarray(L0_ffn_W, f32).T),
        "v_r": asc(np.asarray(L0_ffn_V_r, f32)),
        "c_T": asc(np.asarray(L0_ffn_C, f32).T),
        "eye2": asc(np.concatenate([2.0 * np.eye(128, dtype=f32),
                                    np.eye(128, dtype=f32)], 1)),
        "in_bv_hbm": asc(np.asarray(L0_attn_in_b, f32).reshape(16, 128).T),
        "norm_wv_hbm": asc(np.asarray(L0_attn_norm_w, f32).reshape(DC, 128).T),
        "norm_bv_hbm": asc(np.asarray(L0_attn_norm_b, f32).reshape(DC, 128).T),
        "out_bv_hbm": asc(np.asarray(L0_attn_out_b, f32).reshape(DC, 128).T),
        "ffn_bv_hbm": asc(np.asarray(L0_ffn_b, f32).reshape(DC, 128).T),
        "g_bv_hbm": asc(np.stack([np.asarray(L0_attn_dt_b, f32),
                              np.asarray(L0_attn_b_b, f32),
                              np.asarray(L0_attn_c_b, f32)], 1)),
        "conv_wv_hbm": asc(np.asarray(L0_attn_conv_w, f32)[:, 0, :]
                       .reshape(DC, 128, 3).transpose(1, 0, 2).reshape(128, 3 * DC)),
        "conv_bv_hbm": asc(np.asarray(L0_attn_conv_b, f32).reshape(DC, 128).T),
        "d_v_hbm": asc(np.asarray(L0_attn_d, f32).reshape(DC, 128).T),
        "alog_v_hbm": asc(np.asarray(L0_attn_a_log, f32).reshape(S, 1)),
    }

    in_maps = []
    for core in range(NCORES):
        m = dict(shared)
        m["x"] = asc(x[core * BL:(core + 1) * BL].reshape(TOK, D).T)
        in_maps.append(m)

    res = run_bass_kernel_spmd(nc, in_maps, core_ids=list(range(NCORES)))
    outs = []
    for core in range(NCORES):
        yc = np.asarray(res.results[core]["y"])
        outs.append(yc.T.reshape(BL, T, D))
    return np.concatenate(outs, 0).astype(np.float32)


# version 08
# speedup vs baseline: 1.4984x; 1.4984x over previous
import numpy as np
import concourse.bass as bass
import concourse.mybir as mybir
from concourse.bass_utils import run_bass_kernel_spmd

F32 = mybir.dt.float32
BF16 = mybir.dt.bfloat16
AF = mybir.ActivationFunctionType
ALU = mybir.AluOpType

NCORES = 8
B, T, D, S = 16, 2048, 1024, 64
BL = B // NCORES
TOK = BL * T
C = 512
NCH = TOK // C
CPS = T // C
DC = D // 128
EPS = 1e-5


class Sem:
    def __init__(self, h):
        self.h = h
        self.n = 0


class Stream:
    def __init__(self, name):
        self.name = name
        self.ops = []
        self.hw = {}

    def emit(self, fn):
        self.ops.append(fn)

    def wait(self, sem, val):
        if not val or val <= self.hw.get(id(sem), 0):
            return
        self.hw[id(sem)] = val
        self.ops.append(lambda e, s=sem.h, v=val: e.wait_ge(s, v))

    def inc(self, sem, k, fn):
        sem.n += k
        self.ops.append(lambda e, f=fn, s=sem.h, kk=k: f(e).then_inc(s, kk))
        return sem.n

    def replay(self, e):
        for f in self.ops:
            f(e)


class Bank:
    def __init__(self, idx):
        self.idx = idx
        self.rel = []

    def acquire(self, stream):
        for sem, val in self.rel:
            stream.wait(sem, val)
        self.rel = []

    def release(self, sem, val):
        self.rel.append((sem, val))


def build_program():
    nc = bass.Bass(trn_type="TRN2", target_bir_lowering=False,
                   detect_race_conditions=False)

    x_d = nc.dram_tensor("x", [D, TOK], F32, kind="ExternalInput").ap()
    y_d = nc.dram_tensor("y", [D, TOK], F32, kind="ExternalOutput").ap()
    inw_d = nc.dram_tensor("in_wT", [D, 2 * D], F32, kind="ExternalInput").ap()
    gw_d = nc.dram_tensor("g_wT", [D, 3 * S], F32, kind="ExternalInput").ap()
    ow_d = nc.dram_tensor("out_wT", [D + S, D], F32, kind="ExternalInput").ap()
    fw_d = nc.dram_tensor("ffn_wT", [D, D], F32, kind="ExternalInput").ap()
    vr_d = nc.dram_tensor("v_r", [D, 32], F32, kind="ExternalInput").ap()
    ct_d = nc.dram_tensor("c_T", [32, D], F32, kind="ExternalInput").ap()
    eye_d = nc.dram_tensor("eye2", [128, 256], F32, kind="ExternalInput").ap()
    inb_d = nc.dram_tensor("in_bv_hbm", [128, 16], F32, kind="ExternalInput").ap()
    nw_d = nc.dram_tensor("norm_wv_hbm", [128, DC], F32, kind="ExternalInput").ap()
    nb_d = nc.dram_tensor("norm_bv_hbm", [128, DC], F32, kind="ExternalInput").ap()
    ob_d = nc.dram_tensor("out_bv_hbm", [128, DC], F32, kind="ExternalInput").ap()
    fb_d = nc.dram_tensor("ffn_bv_hbm", [128, DC], F32, kind="ExternalInput").ap()
    gb_d = nc.dram_tensor("g_bv_hbm", [S, 3], F32, kind="ExternalInput").ap()
    cw_d = nc.dram_tensor("conv_wd_hbm", [128, 3 * DC * 128], F32, kind="ExternalInput").ap()
    cb_d = nc.dram_tensor("conv_bv_hbm", [128, DC], F32, kind="ExternalInput").ap()
    dv_d = nc.dram_tensor("d_v_hbm", [128, DC], F32, kind="ExternalInput").ap()
    al_d = nc.dram_tensor("alog_v_hbm", [S, 1], F32, kind="ExternalInput").ap()

    x_r = x_d.rearrange("(c p) t -> c p t", p=128)
    y_r = y_d.rearrange("(c p) t -> c p t", p=128)

    A = nc.alloc_sbuf_tensor
    in_wb = A("in_wb", [128, DC * 2 * D], BF16)
    g_wb = A("g_wb", [128, DC * 3 * S], BF16)
    out_wb = A("out_wb", [128, DC * D], BF16)
    out_wyb = A("out_wyb", [S, D], BF16)
    ffn_wb = A("ffn_wb", [128, DC * D], BF16)
    vrb = A("vrb", [128, DC * 32], BF16)
    ctb = A("ctb", [32, D], BF16)
    eyeb = A("eyeb", [128, 256], BF16)
    onesn = A("onesn", [128, 128], BF16)
    onesp = A("onesp", [128, 128], BF16)

    in_bv = A("in_bv", [128, 16], F32)
    norm_wv = A("norm_wv", [128, DC], F32)
    norm_bv = A("norm_bv", [128, DC], F32)
    out_bv = A("out_bv", [128, DC], F32)
    ffn_bv = A("ffn_bv", [128, DC], F32)
    g_bv = A("g_bv", [S, 3], F32)
    conv_bv = A("conv_bv", [128, DC], F32)
    d_v = A("d_v", [128, DC], F32)
    alog_v = A("alog_v", [S, 1], F32)
    tmpa = A("tmpa", [S, 1], F32)
    aT = A("aT", [S, 1], F32)
    a1e4T = A("a1e4T", [S, 1], F32)
    epsT = A("epsT", [128, 1], F32)

    x0b = A("x0b", [128, 2 * DC * C], BF16)
    sqt = A("sqt", [128, DC * C], BF16)
    tt = A("tt", [128, DC * C], BF16)
    nbb = A("nbb", [128, DC], BF16)
    in_bv2 = A("in_bv2", [128, 16], F32)
    xp = A("xp", [128, 2 * DC * 514], BF16)
    zt = A("zt", [128, 2 * DC * C], BF16)
    xcr = A("xcr", [128, 2 * C], BF16)
    xconv = A("xconv", [128, DC * C], BF16)
    cwd = A("cwd", [128, 3 * DC * 128], BF16)
    gtA = A("gtA", [S, C], F32)
    gtB = A("gtB", [S, C], F32)
    gtC = A("gtC", [S, C], F32)
    bT = A("bT", [S, C], F32)
    cTt = A("cTt", [S, C], F32)
    states = A("states", [S, C], F32)
    carry = A("carry", [S, 1], F32)
    yB = A("yB", [S, C], BF16)
    uv = A("uv", [128, DC * C], F32)
    ub = A("ub", [128, DC * C], BF16)
    x1b = A("x1b", [128, DC * C], BF16)
    bsr_b = A("bsr_b", [32, C], BF16)
    sqmu1 = A("sqmu1", [1, C], BF16)
    negones1 = A("negones1", [1, 128], BF16)
    lnv = A("lnv", [128, C], F32)
    rs_b = A("rs_b", [128, C], F32)

    ps = [nc.alloc_psum_tensor(f"ps{i}", [128, C], F32) for i in range(8)]
    bank = [Bank(i) for i in range(8)]

    def inw_t(c, m):
        return in_wb.ap()[:, c * 2 * D + m * 128: c * 2 * D + (m + 1) * 128]

    def gw_t(c, g):
        return g_wb.ap()[:, c * 3 * S + g * S: c * 3 * S + (g + 1) * S]

    def ow_t(c, m):
        return out_wb.ap()[:, c * D + m * 128: c * D + (m + 1) * 128]

    def fw_t(c, m):
        return ffn_wb.ap()[:, c * D + m * 128: c * D + (m + 1) * 128]

    def x0_t(j, c):
        return x0b.ap()[:, (j * DC + c) * C: (j * DC + c + 1) * C]

    def ch(tile, c):
        return tile.ap()[:, c * C: (c + 1) * C]

    def xp_t(j, c, lo, hi):
        base = (j * DC + c) * 514
        return xp.ap()[:, base + lo: base + hi]

    def z_t(j, c):
        return zt.ap()[:, (j * DC + c) * C: (j * DC + c + 1) * C]

    def cwd_t(c, k):
        return cwd.ap()[:, (c * 3 + k) * 128:(c * 3 + k + 1) * 128]

    def xcr_t(c):
        return xcr.ap()[:, (c % 2) * C:(c % 2 + 1) * C]

    with (
        nc.semaphore("dma_w") as h_dw,
        nc.semaphore("dma_g") as h_dg,
        nc.semaphore("dma_o") as h_do,
        nc.semaphore("s_se") as h_se,
        nc.semaphore("s_ve") as h_ve,
        nc.semaphore("s_te") as h_te,
        nc.Block() as block,
    ):
        dw, dg, do = Sem(h_dw), Sem(h_dg), Sem(h_do)
        se, ve, te = Sem(h_se), Sem(h_ve), Sem(h_te)
        Ssy, Sgp, Sse, Sve, Ste = (Stream("sync"), Stream("gpsimd"),
                                   Stream("scalar"), Stream("vector"),
                                   Stream("tensor"))

        x0_ld_list = [0] * NCH

        def emit_x0_load(i):
            t0 = i * C
            x0_ld_list[i] = Sgp.inc(dg, 16, lambda e, j=i % 2, t0_=t0: e.dma_start(
                out=x0b.ap()[:, j * DC * C:(j + 1) * DC * C]
                    .rearrange("p (c t) -> p c t", c=DC),
                in_=x_d.rearrange("(c p) t -> p c t", p=128)[:, :, t0_: t0_ + C]))

        a2_done = [0] * NCH
        musq1_mm = [0] * NCH
        musq2_mm = [0] * NCH
        musq3_mm = [0] * NCH
        rsA_done = [0] * NCH
        v2_done = [[0] * DC for _ in range(NCH)]
        t_done = [[0] * DC for _ in range(NCH)]
        h_done = [[0] * 16 for _ in range(NCH)]
        hevac = [[0] * 16 for _ in range(NCH)]
        edge_r = [[0] * DC for _ in range(NCH)]
        gproj_done = [0] * NCH
        conv_done = [[0] * DC for _ in range(NCH)]
        prev_tanh = [0] * NCH
        xcd_done = [[0] * DC for _ in range(NCH)]
        op_done = [[0] * DC for _ in range(NCH)]
        uev = [[0] * DC for _ in range(NCH)]
        ln2s_done = [0] * NCH
        rs2_done = [0] * NCH
        x1_done = [[0] * DC for _ in range(NCH)]
        ffn_last = [0] * NCH
        vev = [[0] * DC for _ in range(NCH)]
        ln3s_done = [0] * NCH
        rs3_done = [0] * NCH
        out_done = [0] * NCH
        x0_ld = [0] * NCH

        for dram, sb in [(inb_d, in_bv), (nw_d, norm_wv), (nb_d, norm_bv),
                         (ob_d, out_bv), (fb_d, ffn_bv), (gb_d, g_bv),
                         (cb_d, conv_bv), (dv_d, d_v),
                         (al_d, alog_v)]:
            Ssy.inc(dw, 16, lambda e, d_=dram, s_=sb: e.dma_start(out=s_.ap(), in_=d_))
        n_vec_loads = dw.n

        Sgp.inc(dg, 16, lambda e: e.dma_start(out=eyeb.ap(), in_=eye_d))
        inw_ld = Sgp.inc(dg, 16, lambda e: e.dma_start(
            out=in_wb.ap().rearrange("p (c m) -> p c m", c=DC),
            in_=inw_d.rearrange("(c p) m -> p c m", p=128)))
        emit_x0_load(0)
        emit_x0_load(1)
        cwd_ld = Sgp.inc(dg, 16, lambda e: e.dma_start(out=cwd.ap(), in_=cw_d))
        gw_ld = Sgp.inc(dg, 16, lambda e: e.dma_start(
            out=g_wb.ap().rearrange("p (c m) -> p c m", c=DC),
            in_=gw_d.rearrange("(c p) m -> p c m", p=128)))
        Sgp.inc(dg, 16, lambda e: e.dma_start(
            out=out_wb.ap().rearrange("p (c m) -> p c m", c=DC),
            in_=ow_d[0:D, :].rearrange("(c p) m -> p c m", p=128)))
        ow_ld = Sgp.inc(dg, 16, lambda e: e.dma_start(
            out=out_wyb.ap(), in_=ow_d[D:D + S, :]))
        Sgp.inc(dg, 16, lambda e: e.dma_start(
            out=ffn_wb.ap().rearrange("p (c m) -> p c m", c=DC),
            in_=fw_d.rearrange("(c p) m -> p c m", p=128)))
        Sgp.inc(dg, 16, lambda e: e.dma_start(
            out=vrb.ap().rearrange("p (c m) -> p c m", c=DC),
            in_=vr_d.rearrange("(c p) m -> p c m", p=128)))
        ffnw_ld = Sgp.inc(dg, 16, lambda e: e.dma_start(out=ctb.ap(), in_=ct_d))

        Sve.emit(lambda e: e.memset(onesn.ap(), -1.0 / D))
        Sve.emit(lambda e: e.memset(negones1.ap(), -1.0))
        Sve.emit(lambda e: e.memset(onesp.ap(), 1.0 / D))
        memset_done = Sve.inc(ve, 1, lambda e: e.memset(epsT.ap(), EPS))
        Sse.wait(dw, n_vec_loads)
        a_exp = Sse.inc(se, 1, lambda e: e.activation(
            out=tmpa.ap(), in_=alog_v.ap(), func=AF.Exp))
        Sve.wait(se, a_exp)
        Sve.emit(lambda e: e.tensor_scalar_mul(aT.ap(), tmpa.ap(), -1.0))
        a_prep = Sve.inc(ve, 1, lambda e: e.tensor_scalar_mul(
            a1e4T.ap(), aT.ap(), 1e-4))
        nbb_done = Sse.inc(se, 1, lambda e: e.activation(
            out=nbb.ap(), in_=norm_bv.ap(), func=AF.Copy))
        Ste.wait(dg, inw_ld)
        Ste.wait(se, nbb_done)
        mv_evac = [0, 0]
        for m in range(16):
            Ste.wait(se, mv_evac[m % 2])
            for c in range(DC):
                fn = lambda e, c_=c, m_=m: e.matmul(
                    ps[m_ % 2].ap()[:, 0:1], inw_t(c_, m_),
                    nbb.ap()[:, c_:c_ + 1],
                    start=(c_ == 0), stop=(c_ == DC - 1))
                if c == DC - 1:
                    mv = Ste.inc(te, 1, fn)
                else:
                    Ste.emit(fn)
            Sse.wait(te, mv)
            mv_evac[m % 2] = Sse.inc(se, 1, lambda e, m_=m: e.activation(
                out=in_bv2.ap()[:, m_:m_ + 1], in_=ps[m_ % 2].ap()[:, 0:1],
                func=AF.Identity, bias=in_bv.ap()[:, m_:m_ + 1]))
        bank[0].release(se, mv_evac[0])
        bank[1].release(se, mv_evac[1])

        sq_ready = [0] * NCH

        def emit_s1(i):
            j = i % 2
            Sse.wait(dg, x0_ld_list[i])
            if i >= 2:
                Sse.wait(te, ln3s_done[i - 2])
            for c in range(DC):
                fn = lambda e, c_=c: e.activation(
                    out=ch(sqt, c_), in_=x0_t(j, c_), func=AF.Square)
                if c == DC - 1:
                    sq_ready[i] = Sse.inc(se, 1, fn)
                else:
                    Sse.emit(fn)

        def stage_a_stats1(i):
            j = i % 2
            if not sq_ready[i]:
                emit_s1(i)
            Ste.wait(dg, x0_ld_list[i])
            Ste.wait(ve, memset_done)
            bank[4].acquire(Ste)
            for c in range(DC):
                fn = lambda e, c_=c: e.matmul(
                    ps[4].ap(), onesn.ap(), x0_t(j, c_),
                    start=(c_ == 0), stop=(c_ == DC - 1))
                if c == DC - 1:
                    a1_done[i] = Ste.inc(te, 1, fn)
                else:
                    Ste.emit(fn)

        def stage_a_stats2(i):
            li = i % CPS
            j = i % 2
            Ste.wait(se, sq_ready[i])
            bank[5].acquire(Ste)
            for c in range(DC):
                fn = lambda e, c_=c: e.matmul(
                    ps[5].ap(), onesp.ap(), ch(sqt, c_),
                    start=(c_ == 0), stop=False, skip_group_check=True)
                if c == DC - 1:
                    a2_done[i] = Ste.inc(te, 1, fn)
                else:
                    Ste.emit(fn)
            Sse.wait(te, a1_done[i])
            if i >= 2:
                Sse.wait(te, musq3_mm[i - 2])
            s2 = Sse.inc(se, 1, lambda e: e.activation(
                out=sqmu1.ap(), in_=ps[4].ap()[0:1, :], func=AF.Square))
            Ste.wait(se, s2)
            musq1_mm[i] = Ste.inc(te, 1, lambda e: e.matmul(
                ps[5].ap(), negones1.ap(), sqmu1.ap(), start=False, stop=True,
                skip_group_check=True))
            Sse.wait(te, musq1_mm[i])
            Sse.emit(lambda e: e.activation(out=lnv.ap(), in_=ps[5].ap(),
                                            func=AF.Ln, bias=epsT.ap()))
            if i >= 2:
                Sse.wait(ve, out_done[i - 2])
            rsA_done[i] = Sse.inc(se, 1, lambda e: e.activation(
                out=rs_b.ap(), in_=lnv.ap(), func=AF.Exp, scale=-0.5))
            Sve.wait(se, rsA_done[i])
            if i >= 1:
                Sve.wait(te, h_done[i - 1][15])
            for c in range(DC):
                Sve.emit(lambda e, c_=c: e.tensor_tensor(
                    ps[5].ap(), x0_t(j, c_), ps[4].ap(), ALU.add))
                v2_done[i][c] = Sve.inc(ve, 1, lambda e, c_=c: e.scalar_tensor_tensor(
                    ch(tt, c_), ps[5].ap(), norm_wv.ap()[:, c_:c_ + 1],
                    rs_b.ap(), ALU.mult, ALU.mult))
                t_done[i][c] = v2_done[i][c]
            bank[4].release(ve, v2_done[i][DC - 1])
            bank[5].release(ve, v2_done[i][DC - 1])
            if i >= 2:
                Sse.wait(te, conv_done[i - 2][DC - 1])
            if li == 0:
                for m in range(DC):
                    Sse.emit(lambda e, m_=m: e.activation(
                        out=xp_t(j, m_, 0, 1), in_=epsT.ap(),
                        func=AF.Identity, scale=0.0))
            else:
                for m in range(DC):
                    Sse.emit(lambda e, m_=m: e.activation(
                        out=xp_t(j, m_, 0, 1),
                        in_=xp_t(1 - j, m_, 512, 513), func=AF.Identity))

        def stage_a_h(i, m_lo, m_hi, conv_b=None):
            li = i % CPS
            j = i % 2
            Ste.wait(dg, inw_ld)
            for m in range(m_lo, m_hi):
                bk = bank[m % 2]
                bk.acquire(Ste)
                for c in range(DC):
                    Ste.wait(ve, t_done[i][c])
                    fn = lambda e, c_=c, m_=m: e.matmul(
                        ps[m_ % 2].ap(), inw_t(c_, m_), ch(tt, c_),
                        start=(c_ == 0), stop=(c_ == DC - 1))
                    if c == DC - 1:
                        h_done[i][m] = Ste.inc(te, 1, fn)
                    else:
                        Ste.emit(fn)
                Sse.wait(te, h_done[i][m])
                if m < DC:
                    if i >= 1 and li != 0:
                        Sse.emit(lambda e, m_=m: e.activation(
                            out=xp_t(j, m_, 1, 513), in_=ps[m_ % 2].ap(),
                            func=AF.Identity, bias=in_bv2.ap()[:, m_:m_ + 1]))
                        hevac[i][m] = Sse.inc(se, 1, lambda e, m_=m: e.activation(
                            out=xp_t(1 - j, m_, 513, 514),
                            in_=ps[m_ % 2].ap()[:, 0:1],
                            func=AF.Identity, bias=in_bv2.ap()[:, m_:m_ + 1]))
                        edge_r[i - 1][m] = hevac[i][m]
                    else:
                        hevac[i][m] = Sse.inc(se, 1, lambda e, m_=m: e.activation(
                            out=xp_t(j, m_, 1, 513), in_=ps[m_ % 2].ap(),
                            func=AF.Identity, bias=in_bv2.ap()[:, m_:m_ + 1]))
                else:
                    if i >= 2:
                        Sse.wait(te, gproj_done[i - 2])
                    hevac[i][m] = Sse.inc(se, 1, lambda e, m_=m: e.activation(
                        out=z_t(j, m_ - DC), in_=ps[m_ % 2].ap(),
                        func=AF.Identity, bias=in_bv2.ap()[:, m_:m_ + 1]))
                bk.release(se, hevac[i][m])
                if conv_b is not None and m < DC:
                    emit_conv_te(conv_b, m)

        def emit_conv_te(b, c):
            jb = b % 2
            bki = 2 + c % 2
            Ste.wait(dg, cwd_ld)
            Ste.wait(se, edge_r[b][c])
            Ste.wait(se, hevac[b][c])
            bank[bki].acquire(Ste)
            Ste.emit(lambda e, c_=c, k_=bki: e.matmul(
                ps[k_].ap(), cwd_t(c_, 0), xp_t(jb, c_, 0, 512),
                start=True, stop=False))
            Ste.emit(lambda e, c_=c, k_=bki: e.matmul(
                ps[k_].ap(), cwd_t(c_, 1), xp_t(jb, c_, 1, 513),
                start=False, stop=False))
            conv_done[b][c] = Ste.inc(te, 1, lambda e, c_=c, k_=bki: e.matmul(
                ps[k_].ap(), cwd_t(c_, 2), xp_t(jb, c_, 2, 514),
                start=False, stop=True))
            Sse.wait(te, conv_done[b][c])
            if c >= 2:
                Sse.wait(ve, xcd_done[b][c - 2])
            tanh_done[b][c] = Sse.inc(se, 1, lambda e, c_=c, k_=bki: e.activation(
                out=xcr_t(c_), in_=ps[k_].ap(),
                func=AF.Tanh, bias=conv_bv.ap()[:, c_:c_ + 1]))
            bank[bki].release(se, tanh_done[b][c])
            Sve.wait(se, tanh_done[b][c])
            if c == 0 and b >= 1:
                Sve.wait(te, op_done[b - 1][DC - 1])
            xcd_done[b][c] = Sve.inc(ve, 1, lambda e, c_=c: e.tensor_scalar_mul(
                ch(xconv, c_), xcr_t(c_), d_v.ap()[:, c_:c_ + 1]))

        def stage_b_g(b):
            li = b % CPS
            j = b % 2
            if li == CPS - 1:
                ze = 0
                for m in range(DC):
                    fn = lambda e, m_=m: e.activation(
                        out=xp_t(j, m_, 513, 514), in_=epsT.ap(),
                        func=AF.Identity, scale=0.0)
                    if m == DC - 1:
                        ze = Sse.inc(se, 1, fn)
                    else:
                        Sse.emit(fn)
                for m in range(DC):
                    edge_r[b][m] = ze
            Ste.wait(dg, gw_ld)
            Ste.wait(se, hevac[b][15])
            gd = 0
            for g, bki in [(0, 2), (1, 3), (2, 6)]:
                bank[bki].acquire(Ste)
                for c in range(DC):
                    fn = lambda e, c_=c, g_=g, k_=bki: e.matmul(
                        ps[k_].ap()[0:S, :], gw_t(c_, g_), z_t(j, c_),
                        start=(c_ == 0), stop=(c_ == DC - 1))
                    if c == DC - 1:
                        gd = Ste.inc(te, 1, fn)
                    else:
                        Ste.emit(fn)
            gproj_done[b] = gd

        def stage_b_gates_nle(b):
            Sse.wait(te, gproj_done[b])
            if b >= 1:
                Sse.wait(ve, y_done[b - 1])
            Sse.emit(lambda e: e.activation(
                out=gtA.ap(), in_=ps[2].ap()[0:S, :], func=AF.Abs,
                bias=g_bv.ap()[:, 0:1]))
            g_rel = Sse.inc(se, 1, lambda e: e.activation(
                out=gtC.ap(), in_=ps[2].ap()[0:S, :], func=AF.Relu,
                bias=g_bv.ap()[:, 0:1]))
            bank[2].release(se, g_rel)
            Sse.emit(lambda e: e.activation(
                out=gtB.ap(), in_=gtA.ap(), func=AF.Exp, scale=-1.0))
            gln = Sse.inc(se, 1, lambda e: e.activation(
                out=gtA.ap(), in_=gtB.ap(), func=AF.Ln, bias=1.0))
            Sve.wait(se, gln)
            spt = Sve.inc(ve, 1, lambda e: e.tensor_tensor(
                gtB.ap(), gtC.ap(), gtA.ap(), ALU.add))
            Sse.wait(ve, spt)
            Sse.wait(ve, a_prep)
            dec_done[b] = Sse.inc(se, 1, lambda e: e.activation(
                out=gtC.ap(), in_=gtB.ap(), func=AF.Exp,
                scale=aT.ap(), bias=a1e4T.ap()))

        def stage_b_gates_exp(b):
            li = b % CPS
            bt_done = Sse.inc(se, 1, lambda e: e.activation(
                out=bT.ap(), in_=ps[3].ap()[0:S, :], func=AF.Tanh,
                bias=g_bv.ap()[:, 1:2]))
            bank[3].release(se, bt_done)
            ct_done[b] = Sse.inc(se, 1, lambda e: e.activation(
                out=cTt.ap(), in_=ps[6].ap()[0:S, :], func=AF.Tanh,
                bias=g_bv.ap()[:, 2:3]))
            bank[6].release(se, ct_done[b])
            Sve.wait(se, dec_done[b])
            Sve.wait(se, bt_done)
            ini = 0.0 if li == 0 else carry.ap()
            Sve.emit(lambda e, ini_=ini: e.tensor_tensor_scan(
                out=states.ap(), data0=gtC.ap(), data1=bT.ap(),
                initial=ini_, op0=ALU.mult, op1=ALU.add))
            Sve.emit(lambda e: e.tensor_copy(carry.ap(), states.ap()[:, C - 1:C]))
            Sve.wait(se, ct_done[b])
            if b >= 1:
                Sve.wait(te, op_done[b - 1][DC - 1])
            y_done[b] = Sve.inc(ve, 1, lambda e: e.tensor_tensor(
                yB.ap(), cTt.ap(), states.ap(), ALU.mult))

        def stage_b_op(b):
            j = b % 2
            Ste.wait(dg, ow_ld)
            for m in range(DC):
                bk = bank[2 + m % 2]
                bk.acquire(Ste)
                for c in range(DC):
                    Ste.wait(ve, xcd_done[b][c])
                    Ste.emit(lambda e, c_=c, m_=m: e.matmul(
                        ps[2 + m_ % 2].ap(), ow_t(c_, m_), ch(xconv, c_),
                        start=(c_ == 0), stop=False))
                Ste.wait(ve, y_done[b])
                Ste.emit(lambda e, m_=m: e.matmul(
                    ps[2 + m_ % 2].ap(), out_wyb.ap()[:, m_ * 128:(m_ + 1) * 128],
                    yB.ap(), start=False, stop=False))
                op_done[b][m] = Ste.inc(te, 1, lambda e, m_=m: e.matmul(
                    ps[2 + m_ % 2].ap(), eyeb.ap()[:, 0:128], x0_t(j, m_),
                    start=False, stop=True))
                Sse.wait(te, op_done[b][m])
                if m == 0:
                    if b >= 1:
                        Sse.wait(do, 16 * b)
                        Sse.wait(te, ln3s_done[b - 1])
                    Sse.wait(te, a2_done[b + 1] if b + 1 < NCH else a2_done[b])
                Sse.emit(lambda e, m_=m: e.activation(
                    out=ch(uv, m_), in_=ps[2 + m_ % 2].ap(), func=AF.Identity,
                    bias=out_bv.ap()[:, m_:m_ + 1]))
                Sse.emit(lambda e, m_=m: e.activation(
                    out=ch(sqt, m_), in_=ps[2 + m_ % 2].ap(), func=AF.Square,
                    bias=out_bv.ap()[:, m_:m_ + 1]))
                uev[b][m] = Sse.inc(se, 1, lambda e, m_=m: e.activation(
                    out=ch(ub, m_), in_=ps[2 + m_ % 2].ap(), func=AF.Identity,
                    bias=out_bv.ap()[:, m_:m_ + 1]))
                bk.release(se, uev[b][m])
            bank[6].acquire(Ste)
            for c in range(DC):
                Ste.wait(se, uev[b][c])
                fn = lambda e, c_=c: e.matmul(
                    ps[6].ap(), onesn.ap(), ch(ub, c_),
                    start=(c_ == 0), stop=(c_ == DC - 1))
                if c == DC - 1:
                    mu2_done[b] = Ste.inc(te, 1, fn)
                else:
                    Ste.emit(fn)
            bank[7].acquire(Ste)
            for c in range(DC):
                fn = lambda e, c_=c: e.matmul(
                    ps[7].ap(), onesp.ap(), ch(sqt, c_),
                    start=(c_ == 0), stop=False, skip_group_check=True)
                if c == DC - 1:
                    ln2s_done[b] = Ste.inc(te, 1, fn)
                else:
                    Ste.emit(fn)

        def stage_b_tail1(b):
            Sse.wait(te, mu2_done[b])
            if b + 1 < NCH:
                Sse.wait(te, musq1_mm[b + 1])
            sm2 = Sse.inc(se, 1, lambda e: e.activation(
                out=sqmu1.ap(), in_=ps[6].ap()[0:1, :], func=AF.Square))
            Ste.wait(se, sm2)
            musq2_mm[b] = Ste.inc(te, 1, lambda e: e.matmul(
                ps[7].ap(), negones1.ap(), sqmu1.ap(), start=False, stop=True,
                skip_group_check=True))
            Sse.wait(te, musq2_mm[b])
            Sse.emit(lambda e: e.activation(out=lnv.ap(), in_=ps[7].ap(),
                                            func=AF.Ln, bias=epsT.ap()))
            if b + 1 < NCH:
                Sse.wait(ve, v2_done[b + 1][DC - 1])
            rs2_done[b] = Sse.inc(se, 1, lambda e: e.activation(
                out=rs_b.ap(), in_=lnv.ap(), func=AF.Exp, scale=-0.5))
            Sve.wait(se, rs2_done[b])
            if b >= 1:
                Sve.wait(te, ffn_last[b - 1])
            for c in range(DC):
                Sve.emit(lambda e, c_=c: e.tensor_tensor(
                    ps[7].ap(), ch(uv, c_), ps[6].ap(), ALU.add))
                x1_done[b][c] = Sve.inc(ve, 1, lambda e, c_=c: e.tensor_tensor(
                    ch(x1b, c_), ps[7].ap(), rs_b.ap(), ALU.mult))
            bank[6].release(ve, x1_done[b][DC - 1])
            bank[7].release(ve, x1_done[b][DC - 1])

        def stage_b_tail2(b):
            j = b % 2
            Ste.wait(dg, ffnw_ld)
            bank[6].acquire(Ste)
            bsr_done = 0
            for c in range(DC):
                Ste.wait(ve, x1_done[b][c])
                fn = lambda e, c_=c: e.matmul(
                    ps[6].ap()[0:32, :], vrb.ap()[:, c_ * 32:(c_ + 1) * 32],
                    ch(x1b, c_), start=(c_ == 0), stop=(c_ == DC - 1))
                if c == DC - 1:
                    bsr_done = Ste.inc(te, 1, fn)
                else:
                    Ste.emit(fn)
            Sse.wait(te, bsr_done)
            s12 = Sse.inc(se, 1, lambda e: e.activation(
                out=bsr_b.ap(), in_=ps[6].ap()[0:32, :], func=AF.Identity))
            bank[6].release(se, s12)
            for m in range(DC):
                bk = bank[2 + m % 2]
                bk.acquire(Ste)
                for c in range(DC):
                    Ste.emit(lambda e, c_=c, m_=m: e.matmul(
                        ps[2 + m_ % 2].ap(), fw_t(c_, m_), ch(x1b, c_),
                        start=(c_ == 0), stop=False))
                Ste.wait(se, s12)
                Ste.emit(lambda e, m_=m: e.matmul(
                    ps[2 + m_ % 2].ap(), ctb.ap()[:, m_ * 128:(m_ + 1) * 128],
                    bsr_b.ap(), start=False, stop=False))
                f_done = Ste.inc(te, 1, lambda e, m_=m: e.matmul(
                    ps[2 + m_ % 2].ap(), eyeb.ap()[:, 128:256], ch(x1b, m_),
                    start=False, stop=True))
                if m == DC - 1:
                    ffn_last[b] = f_done
                Sse.wait(te, f_done)
                Sse.wait(te, ln2s_done[b])
                Sse.wait(ve, x1_done[b][m])
                Sse.emit(lambda e, m_=m: e.activation(
                    out=ch(uv, m_), in_=ps[2 + m_ % 2].ap(), func=AF.Identity,
                    bias=ffn_bv.ap()[:, m_:m_ + 1]))
                Sse.emit(lambda e, m_=m: e.activation(
                    out=ch(sqt, m_), in_=ps[2 + m_ % 2].ap(), func=AF.Square,
                    bias=ffn_bv.ap()[:, m_:m_ + 1]))
                vev[b][m] = Sse.inc(se, 1, lambda e, m_=m: e.activation(
                    out=ch(ub, m_), in_=ps[2 + m_ % 2].ap(), func=AF.Identity,
                    bias=ffn_bv.ap()[:, m_:m_ + 1]))
                bk.release(se, vev[b][m])
            bank[6].acquire(Ste)
            for c in range(DC):
                Ste.wait(se, vev[b][c])
                fn = lambda e, c_=c: e.matmul(
                    ps[6].ap(), onesn.ap(), ch(ub, c_),
                    start=(c_ == 0), stop=(c_ == DC - 1))
                if c == DC - 1:
                    mu3 = Ste.inc(te, 1, fn)
                else:
                    Ste.emit(fn)
            bank[7].acquire(Ste)
            for c in range(DC):
                fn = lambda e, c_=c: e.matmul(
                    ps[7].ap(), onesp.ap(), ch(sqt, c_),
                    start=(c_ == 0), stop=False, skip_group_check=True)
                if c == DC - 1:
                    ln3s_done[b] = Ste.inc(te, 1, fn)
                else:
                    Ste.emit(fn)
            Sse.wait(te, mu3)
            Sse.wait(te, musq2_mm[b])
            sm3 = Sse.inc(se, 1, lambda e: e.activation(
                out=sqmu1.ap(), in_=ps[6].ap()[0:1, :], func=AF.Square))
            Ste.wait(se, sm3)
            musq3_mm[b] = Ste.inc(te, 1, lambda e: e.matmul(
                ps[7].ap(), negones1.ap(), sqmu1.ap(), start=False, stop=True,
                skip_group_check=True))
            Sse.wait(te, musq3_mm[b])
            Sse.emit(lambda e: e.activation(out=lnv.ap(), in_=ps[7].ap(),
                                            func=AF.Ln, bias=epsT.ap()))
            Sse.wait(ve, x1_done[b][DC - 1])
            rs3_done[b] = Sse.inc(se, 1, lambda e: e.activation(
                out=rs_b.ap(), in_=lnv.ap(), func=AF.Exp, scale=-0.5))
            Sve.wait(se, rs3_done[b])
            for c in range(DC):
                Sve.emit(lambda e, c_=c: e.tensor_tensor(
                    ps[7].ap(), ch(uv, c_), ps[6].ap(), ALU.add))
                fn = lambda e, c_=c: e.tensor_tensor(
                    ch(uv, c_), ps[7].ap(), rs_b.ap(), ALU.mult)
                if c == DC - 1:
                    out_done[b] = Sve.inc(ve, 1, fn)
                else:
                    Sve.emit(fn)
            bank[6].release(ve, out_done[b])
            bank[7].release(ve, out_done[b])
            Ssy.wait(ve, out_done[b])
            t0 = b * C
            Ssy.inc(do, 16, lambda e, t0_=t0: e.dma_start(
                out=y_d.rearrange("(c p) t -> p c t", p=128)[:, :, t0_:t0_ + C],
                in_=uv.ap().rearrange("p (c t) -> p c t", c=DC)))
            if b + 2 < NCH:
                Sgp.wait(te, op_done[b][DC - 1])
                emit_x0_load(b + 2)
                emit_s1(b + 2)

        a1_done = [0] * NCH
        mu2_done = [0] * NCH
        ct_done = [0] * NCH
        dec_done = [0] * NCH
        y_done = [0] * NCH
        tanh_done = [[0] * DC for _ in range(NCH)]
        for i in range(NCH + 1):
            b = i - 1
            if i < NCH:
                stage_a_stats1(i)
            if b >= 0:
                stage_b_g(b)
            if i < NCH:
                stage_a_stats2(i)
            if b >= 0:
                stage_b_gates_nle(b)
                stage_b_gates_exp(b)
            if i < NCH:
                stage_a_h(i, 0, DC, conv_b=b if b >= 0 else None)
            elif b >= 0:
                for c in range(DC):
                    emit_conv_te(b, c)
            if b >= 0:
                stage_b_op(b)
                stage_b_tail1(b)
            if i < NCH:
                stage_a_h(i, DC, 16)
            if b >= 0:
                stage_b_tail2(b)

        final = do.n
        Ssy.emit(lambda e, v=final: e.wait_ge(do.h, v))

        block.sync(Ssy.replay)
        block.gpsimd(Sgp.replay)
        block.scalar(Sse.replay)
        block.vector(Sve.replay)
        block.tensor(Ste.replay)

    return nc


_NC = None


def _conv_diag(conv_w):
    out = np.zeros((128, 3 * DC * 128), dtype=np.float32)
    idx = np.arange(128)
    for c in range(DC):
        for k in range(3):
            out[idx, (c * 3 + k) * 128 + idx] = conv_w[c * 128 + idx, 0, k]
    return out


def kernel(L0_attn_x, L0_attn_norm_w, L0_attn_norm_b, L0_attn_in_w, L0_attn_in_b,
           L0_attn_conv_w, L0_attn_conv_b, L0_attn_dt_w, L0_attn_dt_b,
           L0_attn_b_w, L0_attn_b_b, L0_attn_c_w, L0_attn_c_b, L0_attn_a_log,
           L0_attn_d, L0_attn_out_w, L0_attn_out_b, L0_ffn_W, L0_ffn_b,
           L0_ffn_V_r, L0_ffn_C):
    global _NC
    if _NC is None:
        _NC = build_program()
    nc = _NC

    f32 = np.float32
    asc = np.ascontiguousarray
    x = np.asarray(L0_attn_x, dtype=f32)

    shared = {
        "in_wT": asc(np.asarray(L0_attn_in_w, f32).T),
        "g_wT": asc(np.concatenate([np.asarray(L0_attn_dt_w, f32),
                                    np.asarray(L0_attn_b_w, f32),
                                    np.asarray(L0_attn_c_w, f32)], 0).T),
        "out_wT": asc(np.asarray(L0_attn_out_w, f32).T),
        "ffn_wT": asc(np.asarray(L0_ffn_W, f32).T),
        "v_r": asc(np.asarray(L0_ffn_V_r, f32)),
        "c_T": asc(np.asarray(L0_ffn_C, f32).T),
        "eye2": asc(np.concatenate([2.0 * np.eye(128, dtype=f32),
                                    np.eye(128, dtype=f32)], 1)),
        "in_bv_hbm": asc(np.asarray(L0_attn_in_b, f32).reshape(16, 128).T),
        "norm_wv_hbm": asc(np.asarray(L0_attn_norm_w, f32).reshape(DC, 128).T),
        "norm_bv_hbm": asc(np.asarray(L0_attn_norm_b, f32).reshape(DC, 128).T),
        "out_bv_hbm": asc(np.asarray(L0_attn_out_b, f32).reshape(DC, 128).T),
        "ffn_bv_hbm": asc(np.asarray(L0_ffn_b, f32).reshape(DC, 128).T),
        "g_bv_hbm": asc(np.stack([np.asarray(L0_attn_dt_b, f32),
                              np.asarray(L0_attn_b_b, f32),
                              np.asarray(L0_attn_c_b, f32)], 1)),
        "conv_wd_hbm": _conv_diag(np.asarray(L0_attn_conv_w, f32)),
        "conv_bv_hbm": asc(np.asarray(L0_attn_conv_b, f32).reshape(DC, 128).T),
        "d_v_hbm": asc(np.asarray(L0_attn_d, f32).reshape(DC, 128).T),
        "alog_v_hbm": asc(np.asarray(L0_attn_a_log, f32).reshape(S, 1)),
    }

    in_maps = []
    for core in range(NCORES):
        m = dict(shared)
        m["x"] = asc(x[core * BL:(core + 1) * BL].reshape(TOK, D).T)
        in_maps.append(m)

    res = run_bass_kernel_spmd(nc, in_maps, core_ids=list(range(NCORES)))
    outs = []
    for core in range(NCORES):
        yc = np.asarray(res.results[core]["y"])
        outs.append(yc.T.reshape(BL, T, D))
    return np.concatenate(outs, 0).astype(np.float32)


# version 13
# speedup vs baseline: 1.5959x; 1.0651x over previous
import numpy as np
import concourse.bass as bass
import concourse.mybir as mybir
from concourse.bass_utils import run_bass_kernel_spmd

F32 = mybir.dt.float32
BF16 = mybir.dt.bfloat16
AF = mybir.ActivationFunctionType
ALU = mybir.AluOpType

NCORES = 8
B, T, D, S = 16, 2048, 1024, 64
BL = B // NCORES
TOK = BL * T
C = 512
NCH = TOK // C
CPS = T // C
DC = D // 128
EPS = 1e-5


class Sem:
    def __init__(self, h):
        self.h = h
        self.n = 0


class Stream:
    def __init__(self, name):
        self.name = name
        self.ops = []
        self.hw = {}

    def emit(self, fn):
        self.ops.append(fn)

    def wait(self, sem, val):
        if not val or val <= self.hw.get(id(sem), 0):
            return
        self.hw[id(sem)] = val
        self.ops.append(lambda e, s=sem.h, v=val: e.wait_ge(s, v))

    def inc(self, sem, k, fn):
        sem.n += k
        self.ops.append(lambda e, f=fn, s=sem.h, kk=k: f(e).then_inc(s, kk))
        return sem.n

    def replay(self, e):
        for f in self.ops:
            f(e)


class Bank:
    def __init__(self, idx):
        self.idx = idx
        self.rel = []

    def acquire(self, stream):
        for sem, val in self.rel:
            stream.wait(sem, val)
        self.rel = []

    def release(self, sem, val):
        self.rel.append((sem, val))


def build_program():
    nc = bass.Bass(trn_type="TRN2", target_bir_lowering=False,
                   detect_race_conditions=False)

    x_d = nc.dram_tensor("x", [D, TOK], F32, kind="ExternalInput").ap()
    y_d = nc.dram_tensor("y", [D, TOK], F32, kind="ExternalOutput").ap()
    inw_d = nc.dram_tensor("in_wT", [D, 2 * D], F32, kind="ExternalInput").ap()
    gw_d = nc.dram_tensor("g_wT", [D, 3 * S], F32, kind="ExternalInput").ap()
    ow_d = nc.dram_tensor("out_wT", [D + S, D], F32, kind="ExternalInput").ap()
    fw_d = nc.dram_tensor("ffn_wT", [D, D], F32, kind="ExternalInput").ap()
    vr_d = nc.dram_tensor("v_r", [D, 32], F32, kind="ExternalInput").ap()
    ct_d = nc.dram_tensor("c_T", [32, D], F32, kind="ExternalInput").ap()
    eye_d = nc.dram_tensor("eye2", [128, 256], F32, kind="ExternalInput").ap()
    inb_d = nc.dram_tensor("in_bv_hbm", [128, 16], F32, kind="ExternalInput").ap()
    nw_d = nc.dram_tensor("norm_wv_hbm", [128, DC], F32, kind="ExternalInput").ap()
    nb_d = nc.dram_tensor("norm_bv_hbm", [128, DC], F32, kind="ExternalInput").ap()
    ob_d = nc.dram_tensor("out_bv_hbm", [128, DC], F32, kind="ExternalInput").ap()
    fb_d = nc.dram_tensor("ffn_bv_hbm", [128, DC], F32, kind="ExternalInput").ap()
    gb_d = nc.dram_tensor("g_bv_hbm", [S, 3], F32, kind="ExternalInput").ap()
    cw_d = nc.dram_tensor("conv_wd_hbm", [128, 3 * DC * 128], F32, kind="ExternalInput").ap()
    cb_d = nc.dram_tensor("conv_bv_hbm", [128, DC], F32, kind="ExternalInput").ap()
    dv_d = nc.dram_tensor("d_v_hbm", [128, DC], F32, kind="ExternalInput").ap()
    al_d = nc.dram_tensor("alog_v_hbm", [S, 1], F32, kind="ExternalInput").ap()

    x_r = x_d.rearrange("(c p) t -> c p t", p=128)
    y_r = y_d.rearrange("(c p) t -> c p t", p=128)

    A = nc.alloc_sbuf_tensor
    in_wb = A("in_wb", [128, DC * 2 * D], BF16)
    g_wb = A("g_wb", [128, DC * 3 * S], BF16)
    out_wb = A("out_wb", [128, DC * D], BF16)
    out_wyb = A("out_wyb", [S, D], BF16)
    ffn_wb = A("ffn_wb", [128, DC * D], BF16)
    vrb = A("vrb", [128, DC * 32], BF16)
    ctb = A("ctb", [32, D], BF16)
    eyeb = A("eyeb", [128, 256], BF16)
    onesn = A("onesn", [128, 128], BF16)
    onesp = A("onesp", [128, 128], BF16)

    in_bv = A("in_bv", [128, 16], F32)
    norm_wv = A("norm_wv", [128, DC], F32)
    norm_bv = A("norm_bv", [128, DC], F32)
    out_bv = A("out_bv", [128, DC], F32)
    ffn_bv = A("ffn_bv", [128, DC], F32)
    g_bv = A("g_bv", [S, 3], F32)
    conv_bv = A("conv_bv", [128, DC], F32)
    d_v = A("d_v", [128, DC], F32)
    alog_v = A("alog_v", [S, 1], F32)
    tmpa = A("tmpa", [S, 1], F32)
    aT = A("aT", [S, 1], F32)
    a1e4T = A("a1e4T", [S, 1], F32)
    epsT = A("epsT", [128, 1], F32)

    x0b = A("x0b", [128, 2 * DC * C], BF16)
    sqt = A("sqt", [128, DC * C], BF16)
    tt = A("tt", [128, DC * C], BF16)
    nbb = A("nbb", [128, DC], BF16)
    in_bv2 = A("in_bv2", [128, 16], F32)
    xp = A("xp", [128, 2 * DC * 514], BF16)
    zt = A("zt", [128, 2 * DC * C], BF16)
    xcr = A("xcr", [128, 2 * C], BF16)
    xconv = A("xconv", [128, DC * C], BF16)
    cwd = A("cwd", [128, 3 * DC * 128], BF16)
    gtA = A("gtA", [S, C], F32)
    gtB = A("gtB", [S, C], F32)
    gtC = A("gtC", [S, C], F32)
    bT = A("bT", [S, C], F32)
    cTt = A("cTt", [S, C], F32)
    states = A("states", [S, C], F32)
    carry = A("carry", [S, 1], F32)
    yB = A("yB", [S, C], BF16)
    uv = A("uv", [128, DC * C], F32)
    ub = A("ub", [128, DC * C], BF16)
    x1b = A("x1b", [128, DC * C], BF16)
    bsr_b = A("bsr_b", [32, C], BF16)
    sqmu1 = A("sqmu1", [1, C], BF16)
    negones1 = A("negones1", [1, 128], BF16)
    mu_sb = A("mu_sb", [128, C], F32)
    lnv = A("lnv", [128, C], F32)
    rs_b = A("rs_b", [128, C], F32)

    ps = [nc.alloc_psum_tensor(f"ps{i}", [128, C], F32) for i in range(8)]
    bank = [Bank(i) for i in range(8)]

    def inw_t(c, m):
        return in_wb.ap()[:, c * 2 * D + m * 128: c * 2 * D + (m + 1) * 128]

    def gw_t(c, g):
        return g_wb.ap()[:, c * 3 * S + g * S: c * 3 * S + (g + 1) * S]

    def ow_t(c, m):
        return out_wb.ap()[:, c * D + m * 128: c * D + (m + 1) * 128]

    def fw_t(c, m):
        return ffn_wb.ap()[:, c * D + m * 128: c * D + (m + 1) * 128]

    def x0_t(j, c):
        return x0b.ap()[:, (j * DC + c) * C: (j * DC + c + 1) * C]

    def ch(tile, c):
        return tile.ap()[:, c * C: (c + 1) * C]

    def xp_t(j, c, lo, hi):
        base = (j * DC + c) * 514
        return xp.ap()[:, base + lo: base + hi]

    def z_t(j, c):
        return zt.ap()[:, (j * DC + c) * C: (j * DC + c + 1) * C]

    def cwd_t(c, k):
        return cwd.ap()[:, (c * 3 + k) * 128:(c * 3 + k + 1) * 128]

    def xcr_t(c):
        return xcr.ap()[:, (c % 2) * C:(c % 2 + 1) * C]

    with (
        nc.semaphore("dma_w") as h_dw,
        nc.semaphore("dma_g") as h_dg,
        nc.semaphore("dma_o") as h_do,
        nc.semaphore("s_se") as h_se,
        nc.semaphore("s_ve") as h_ve,
        nc.semaphore("s_te") as h_te,
        nc.Block() as block,
    ):
        dw, dg, do = Sem(h_dw), Sem(h_dg), Sem(h_do)
        se, ve, te = Sem(h_se), Sem(h_ve), Sem(h_te)
        Ssy, Sgp, Sse, Sve, Ste = (Stream("sync"), Stream("gpsimd"),
                                   Stream("scalar"), Stream("vector"),
                                   Stream("tensor"))

        x0_ld_list = [0] * NCH

        def emit_x0_load(i):
            t0 = i * C
            x0_ld_list[i] = Sgp.inc(dg, 16, lambda e, j=i % 2, t0_=t0: e.dma_start(
                out=x0b.ap()[:, j * DC * C:(j + 1) * DC * C]
                    .rearrange("p (c t) -> p c t", c=DC),
                in_=x_d.rearrange("(c p) t -> p c t", p=128)[:, :, t0_: t0_ + C]))

        a2_done = [0] * NCH
        musq1_mm = [0] * NCH
        musq2_mm = [0] * NCH
        musq3_mm = [0] * NCH
        rsA_done = [0] * NCH
        v2_done = [[0] * DC for _ in range(NCH)]
        t_done = [[0] * DC for _ in range(NCH)]
        h_done = [[0] * 16 for _ in range(NCH)]
        hevac = [[0] * 16 for _ in range(NCH)]
        edge_r = [[0] * DC for _ in range(NCH)]
        gproj_done = [0] * NCH
        conv_done = [[0] * DC for _ in range(NCH)]
        prev_tanh = [0] * NCH
        xcd_done = [[0] * DC for _ in range(NCH)]
        op_done = [[0] * DC for _ in range(NCH)]
        uev = [[0] * DC for _ in range(NCH)]
        ln2s_done = [0] * NCH
        rs2_done = [0] * NCH
        x1_done = [[0] * DC for _ in range(NCH)]
        ffn_last = [0] * NCH
        vev = [[0] * DC for _ in range(NCH)]
        ln3s_done = [0] * NCH
        rs3_done = [0] * NCH
        out_done = [0] * NCH
        x0_ld = [0] * NCH

        for dram, sb in [(inb_d, in_bv), (nw_d, norm_wv), (nb_d, norm_bv),
                         (ob_d, out_bv), (fb_d, ffn_bv), (gb_d, g_bv),
                         (cb_d, conv_bv), (dv_d, d_v),
                         (al_d, alog_v)]:
            Ssy.inc(dw, 16, lambda e, d_=dram, s_=sb: e.dma_start(out=s_.ap(), in_=d_))
        n_vec_loads = dw.n

        Sgp.inc(dg, 16, lambda e: e.dma_start(out=eyeb.ap(), in_=eye_d))
        inw_ld = Sgp.inc(dg, 16, lambda e: e.dma_start(
            out=in_wb.ap().rearrange("p (c m) -> p c m", c=DC),
            in_=inw_d.rearrange("(c p) m -> p c m", p=128)))
        emit_x0_load(0)
        emit_x0_load(1)
        cwd_ld = Sgp.inc(dg, 16, lambda e: e.dma_start(out=cwd.ap(), in_=cw_d))
        gw_ld = Sgp.inc(dg, 16, lambda e: e.dma_start(
            out=g_wb.ap().rearrange("p (c m) -> p c m", c=DC),
            in_=gw_d.rearrange("(c p) m -> p c m", p=128)))
        Sgp.inc(dg, 16, lambda e: e.dma_start(
            out=out_wb.ap().rearrange("p (c m) -> p c m", c=DC),
            in_=ow_d[0:D, :].rearrange("(c p) m -> p c m", p=128)))
        ow_ld = Sgp.inc(dg, 16, lambda e: e.dma_start(
            out=out_wyb.ap(), in_=ow_d[D:D + S, :]))
        Sgp.inc(dg, 16, lambda e: e.dma_start(
            out=ffn_wb.ap().rearrange("p (c m) -> p c m", c=DC),
            in_=fw_d.rearrange("(c p) m -> p c m", p=128)))
        Sgp.inc(dg, 16, lambda e: e.dma_start(
            out=vrb.ap().rearrange("p (c m) -> p c m", c=DC),
            in_=vr_d.rearrange("(c p) m -> p c m", p=128)))
        ffnw_ld = Sgp.inc(dg, 16, lambda e: e.dma_start(out=ctb.ap(), in_=ct_d))

        Sve.emit(lambda e: e.memset(onesn.ap(), -1.0 / D))
        Sve.emit(lambda e: e.memset(negones1.ap(), -1.0))
        Sve.emit(lambda e: e.memset(onesp.ap(), 1.0 / D))
        memset_done = Sve.inc(ve, 1, lambda e: e.memset(epsT.ap(), EPS))
        Sse.wait(dw, n_vec_loads)
        a_exp = Sse.inc(se, 1, lambda e: e.activation(
            out=tmpa.ap(), in_=alog_v.ap(), func=AF.Exp))
        Sve.wait(se, a_exp)
        Sve.emit(lambda e: e.tensor_scalar_mul(aT.ap(), tmpa.ap(), -1.0))
        a_prep = Sve.inc(ve, 1, lambda e: e.tensor_scalar_mul(
            a1e4T.ap(), aT.ap(), 1e-4))
        nbb_done = Sse.inc(se, 1, lambda e: e.activation(
            out=nbb.ap(), in_=norm_bv.ap(), func=AF.Copy))
        Ste.wait(dg, inw_ld)
        Ste.wait(se, nbb_done)
        mv_evac = [0, 0]
        for m in range(16):
            Ste.wait(se, mv_evac[m % 2])
            for c in range(DC):
                fn = lambda e, c_=c, m_=m: e.matmul(
                    ps[m_ % 2].ap()[:, 0:1], inw_t(c_, m_),
                    nbb.ap()[:, c_:c_ + 1],
                    start=(c_ == 0), stop=(c_ == DC - 1))
                if c == DC - 1:
                    mv = Ste.inc(te, 1, fn)
                else:
                    Ste.emit(fn)
            Sse.wait(te, mv)
            mv_evac[m % 2] = Sse.inc(se, 1, lambda e, m_=m: e.activation(
                out=in_bv2.ap()[:, m_:m_ + 1], in_=ps[m_ % 2].ap()[:, 0:1],
                func=AF.Identity, bias=in_bv.ap()[:, m_:m_ + 1]))
        bank[0].release(se, mv_evac[0])
        bank[1].release(se, mv_evac[1])

        sq_ready = [0] * NCH

        def emit_s1(i):
            j = i % 2
            Sse.wait(dg, x0_ld_list[i])
            if i >= 2:
                Sse.wait(te, ln3s_done[i - 2])
            for c in range(DC):
                fn = lambda e, c_=c: e.activation(
                    out=ch(sqt, c_), in_=x0_t(j, c_), func=AF.Square)
                if c == DC - 1:
                    sq_ready[i] = Sse.inc(se, 1, fn)
                else:
                    Sse.emit(fn)

        def stage_a_stats1(i):
            j = i % 2
            if not sq_ready[i]:
                emit_s1(i)
            Ste.wait(dg, x0_ld_list[i])
            Ste.wait(ve, memset_done)
            bank[4].acquire(Ste)
            for c in range(DC):
                fn = lambda e, c_=c: e.matmul(
                    ps[4].ap(), onesn.ap(), x0_t(j, c_),
                    start=(c_ == 0), stop=(c_ == DC - 1))
                if c == DC - 1:
                    a1_done[i] = Ste.inc(te, 1, fn)
                else:
                    Ste.emit(fn)

        def stage_a_stats2(i):
            li = i % CPS
            j = i % 2
            Ste.wait(se, sq_ready[i])
            bank[5].acquire(Ste)
            for c in range(DC):
                fn = lambda e, c_=c: e.matmul(
                    ps[5].ap(), onesp.ap(), ch(sqt, c_),
                    start=(c_ == 0), stop=(c_ == DC - 1))
                if c == DC - 1:
                    a2_done[i] = Ste.inc(te, 1, fn)
                else:
                    Ste.emit(fn)
            Sse.wait(te, a1_done[i])
            if i >= 2:
                Sse.wait(te, musq3_mm[i - 2])
            s2 = Sse.inc(se, 1, lambda e: e.activation(
                out=sqmu1.ap(), in_=ps[4].ap()[0:1, :], func=AF.Square))
            Ste.wait(se, s2)
            musq1_mm[i] = Ste.inc(te, 1, lambda e: e.matmul(
                ps[5].ap(), negones1.ap(), sqmu1.ap(), start=False, stop=True,
                skip_group_check=True))
            Sse.wait(te, musq1_mm[i])
            Sse.emit(lambda e: e.activation(out=lnv.ap(), in_=ps[5].ap(),
                                            func=AF.Ln, bias=epsT.ap()))
            if i >= 2:
                Sse.wait(ve, out_done[i - 2])
            rsA_done[i] = Sse.inc(se, 1, lambda e: e.activation(
                out=rs_b.ap(), in_=lnv.ap(), func=AF.Exp, scale=-0.5))
            Sve.wait(se, rsA_done[i])
            if i >= 1:
                Sve.wait(te, h_done[i - 1][15])
            for c in range(DC):
                Sve.emit(lambda e, c_=c: e.tensor_tensor(
                    ps[5].ap(), x0_t(j, c_), ps[4].ap(), ALU.add))
                v2_done[i][c] = Sve.inc(ve, 1, lambda e, c_=c: e.scalar_tensor_tensor(
                    ch(tt, c_), ps[5].ap(), norm_wv.ap()[:, c_:c_ + 1],
                    rs_b.ap(), ALU.mult, ALU.mult))
                t_done[i][c] = v2_done[i][c]
            bank[4].release(ve, v2_done[i][DC - 1])
            bank[5].release(ve, v2_done[i][DC - 1])
            if i >= 2:
                Sse.wait(te, conv_done[i - 2][DC - 1])
            if li == 0:
                for m in range(DC):
                    Sse.emit(lambda e, m_=m: e.activation(
                        out=xp_t(j, m_, 0, 1), in_=epsT.ap(),
                        func=AF.Identity, scale=0.0))
            else:
                for m in range(DC):
                    Sse.emit(lambda e, m_=m: e.activation(
                        out=xp_t(j, m_, 0, 1),
                        in_=xp_t(1 - j, m_, 512, 513), func=AF.Identity))

        def stage_a_h(i, m_lo, m_hi, conv_b=None):
            li = i % CPS
            j = i % 2
            Ste.wait(dg, inw_ld)
            for m in range(m_lo, m_hi):
                bk = bank[m % 2]
                bk.acquire(Ste)
                for c in range(DC):
                    Ste.wait(ve, t_done[i][c])
                    fn = lambda e, c_=c, m_=m: e.matmul(
                        ps[m_ % 2].ap(), inw_t(c_, m_), ch(tt, c_),
                        start=(c_ == 0), stop=(c_ == DC - 1))
                    if c == DC - 1:
                        h_done[i][m] = Ste.inc(te, 1, fn)
                    else:
                        Ste.emit(fn)
                Sse.wait(te, h_done[i][m])
                if m < DC:
                    if i >= 1 and li != 0:
                        Sse.emit(lambda e, m_=m: e.activation(
                            out=xp_t(j, m_, 1, 513), in_=ps[m_ % 2].ap(),
                            func=AF.Identity, bias=in_bv2.ap()[:, m_:m_ + 1]))
                        hevac[i][m] = Sse.inc(se, 1, lambda e, m_=m: e.activation(
                            out=xp_t(1 - j, m_, 513, 514),
                            in_=ps[m_ % 2].ap()[:, 0:1],
                            func=AF.Identity, bias=in_bv2.ap()[:, m_:m_ + 1]))
                        edge_r[i - 1][m] = hevac[i][m]
                    else:
                        hevac[i][m] = Sse.inc(se, 1, lambda e, m_=m: e.activation(
                            out=xp_t(j, m_, 1, 513), in_=ps[m_ % 2].ap(),
                            func=AF.Identity, bias=in_bv2.ap()[:, m_:m_ + 1]))
                else:
                    if i >= 2:
                        Sse.wait(te, gproj_done[i - 2])
                    hevac[i][m] = Sse.inc(se, 1, lambda e, m_=m: e.activation(
                        out=z_t(j, m_ - DC), in_=ps[m_ % 2].ap(),
                        func=AF.Identity, bias=in_bv2.ap()[:, m_:m_ + 1]))
                bk.release(se, hevac[i][m])
                if conv_b is not None and m < DC:
                    emit_conv_te(conv_b, m)

        def emit_conv_te(b, c):
            jb = b % 2
            bki = 2 + c % 2
            Ste.wait(dg, cwd_ld)
            Ste.wait(se, edge_r[b][c])
            Ste.wait(se, hevac[b][c])
            bank[bki].acquire(Ste)
            Ste.emit(lambda e, c_=c, k_=bki: e.matmul(
                ps[k_].ap(), cwd_t(c_, 0), xp_t(jb, c_, 0, 512),
                start=True, stop=False))
            Ste.emit(lambda e, c_=c, k_=bki: e.matmul(
                ps[k_].ap(), cwd_t(c_, 1), xp_t(jb, c_, 1, 513),
                start=False, stop=False))
            conv_done[b][c] = Ste.inc(te, 1, lambda e, c_=c, k_=bki: e.matmul(
                ps[k_].ap(), cwd_t(c_, 2), xp_t(jb, c_, 2, 514),
                start=False, stop=True))
            Sse.wait(te, conv_done[b][c])
            if c >= 2:
                Sse.wait(ve, xcd_done[b][c - 2])
            tanh_done[b][c] = Sse.inc(se, 1, lambda e, c_=c, k_=bki: e.activation(
                out=xcr_t(c_), in_=ps[k_].ap(),
                func=AF.Tanh, bias=conv_bv.ap()[:, c_:c_ + 1]))
            bank[bki].release(se, tanh_done[b][c])
            Sve.wait(se, tanh_done[b][c])
            if c == 0 and b >= 1:
                Sve.wait(te, op_done[b - 1][DC - 1])
            xcd_done[b][c] = Sve.inc(ve, 1, lambda e, c_=c: e.tensor_scalar_mul(
                ch(xconv, c_), xcr_t(c_), d_v.ap()[:, c_:c_ + 1]))

        def stage_b_g(b):
            li = b % CPS
            j = b % 2
            if li == CPS - 1:
                ze = 0
                for m in range(DC):
                    fn = lambda e, m_=m: e.activation(
                        out=xp_t(j, m_, 513, 514), in_=epsT.ap(),
                        func=AF.Identity, scale=0.0)
                    if m == DC - 1:
                        ze = Sse.inc(se, 1, fn)
                    else:
                        Sse.emit(fn)
                for m in range(DC):
                    edge_r[b][m] = ze
            Ste.wait(dg, gw_ld)
            Ste.wait(se, hevac[b][15])
            gd = 0
            for g, bki in [(0, 2), (1, 3), (2, 6)]:
                bank[bki].acquire(Ste)
                for c in range(DC):
                    fn = lambda e, c_=c, g_=g, k_=bki: e.matmul(
                        ps[k_].ap()[0:S, :], gw_t(c_, g_), z_t(j, c_),
                        start=(c_ == 0), stop=(c_ == DC - 1))
                    if c == DC - 1:
                        gd = Ste.inc(te, 1, fn)
                    else:
                        Ste.emit(fn)
            gproj_done[b] = gd

        def stage_b_gates_nle(b):
            Sse.wait(te, gproj_done[b])
            if b >= 1:
                Sse.wait(ve, y_done[b - 1])
            Sse.emit(lambda e: e.activation(
                out=gtA.ap(), in_=ps[2].ap()[0:S, :], func=AF.Abs,
                bias=g_bv.ap()[:, 0:1]))
            g_rel = Sse.inc(se, 1, lambda e: e.activation(
                out=gtC.ap(), in_=ps[2].ap()[0:S, :], func=AF.Relu,
                bias=g_bv.ap()[:, 0:1]))
            bank[2].release(se, g_rel)
            Sse.emit(lambda e: e.activation(
                out=gtB.ap(), in_=gtA.ap(), func=AF.Exp, scale=-1.0))
            gln = Sse.inc(se, 1, lambda e: e.activation(
                out=gtA.ap(), in_=gtB.ap(), func=AF.Ln, bias=1.0))
            Sve.wait(se, gln)
            spt = Sve.inc(ve, 1, lambda e: e.tensor_tensor(
                gtB.ap(), gtC.ap(), gtA.ap(), ALU.add))
            Sse.wait(ve, spt)
            Sse.wait(ve, a_prep)
            dec_done[b] = Sse.inc(se, 1, lambda e: e.activation(
                out=gtC.ap(), in_=gtB.ap(), func=AF.Exp,
                scale=aT.ap(), bias=a1e4T.ap()))

        def stage_b_gates_exp(b):
            li = b % CPS
            bt_done = Sse.inc(se, 1, lambda e: e.activation(
                out=bT.ap(), in_=ps[3].ap()[0:S, :], func=AF.Tanh,
                bias=g_bv.ap()[:, 1:2]))
            bank[3].release(se, bt_done)
            ct_done[b] = Sse.inc(se, 1, lambda e: e.activation(
                out=cTt.ap(), in_=ps[6].ap()[0:S, :], func=AF.Tanh,
                bias=g_bv.ap()[:, 2:3]))
            bank[6].release(se, ct_done[b])
            Sve.wait(se, dec_done[b])
            Sve.wait(se, bt_done)
            ini = 0.0 if li == 0 else carry.ap()
            scan_done = Sve.inc(ve, 1, lambda e, ini_=ini: e.tensor_tensor_scan(
                out=states.ap(), data0=gtC.ap(), data1=bT.ap(),
                initial=ini_, op0=ALU.mult, op1=ALU.add))
            Sve.wait(ve, scan_done)
            Sve.emit(lambda e: e.tensor_copy(carry.ap(), states.ap()[:, C - 1:C]))
            Sve.wait(se, ct_done[b])
            if b >= 1:
                Sve.wait(te, op_done[b - 1][DC - 1])
            y_done[b] = Sve.inc(ve, 1, lambda e: e.tensor_tensor(
                yB.ap(), cTt.ap(), states.ap(), ALU.mult))

        def stage_b_op(b):
            j = b % 2
            Ste.wait(dg, ow_ld)
            for m in range(DC):
                bk = bank[2 + m % 2]
                bk.acquire(Ste)
                for c in range(DC):
                    Ste.wait(ve, xcd_done[b][c])
                    Ste.emit(lambda e, c_=c, m_=m: e.matmul(
                        ps[2 + m_ % 2].ap(), ow_t(c_, m_), ch(xconv, c_),
                        start=(c_ == 0), stop=False))
                Ste.wait(ve, y_done[b])
                Ste.emit(lambda e, m_=m: e.matmul(
                    ps[2 + m_ % 2].ap(), out_wyb.ap()[:, m_ * 128:(m_ + 1) * 128],
                    yB.ap(), start=False, stop=False))
                op_done[b][m] = Ste.inc(te, 1, lambda e, m_=m: e.matmul(
                    ps[2 + m_ % 2].ap(), eyeb.ap()[:, 0:128], x0_t(j, m_),
                    start=False, stop=True))
                Sse.wait(te, op_done[b][m])
                if m == 0:
                    if b >= 1:
                        Sse.wait(do, 16 * b)
                        Sse.wait(te, ln3s_done[b - 1])
                    Sse.wait(te, a2_done[b + 1] if b + 1 < NCH else a2_done[b])
                Sse.emit(lambda e, m_=m: e.activation(
                    out=ch(uv, m_), in_=ps[2 + m_ % 2].ap(), func=AF.Identity,
                    bias=out_bv.ap()[:, m_:m_ + 1]))
                Sse.emit(lambda e, m_=m: e.activation(
                    out=ch(sqt, m_), in_=ps[2 + m_ % 2].ap(), func=AF.Square,
                    bias=out_bv.ap()[:, m_:m_ + 1]))
                uev[b][m] = Sse.inc(se, 1, lambda e, m_=m: e.activation(
                    out=ch(ub, m_), in_=ps[2 + m_ % 2].ap(), func=AF.Identity,
                    bias=out_bv.ap()[:, m_:m_ + 1]))
                bk.release(se, uev[b][m])
            bank[6].acquire(Ste)
            for c in range(DC):
                Ste.wait(se, uev[b][c])
                fn = lambda e, c_=c: e.matmul(
                    ps[6].ap(), onesn.ap(), ch(ub, c_),
                    start=(c_ == 0), stop=(c_ == DC - 1))
                if c == DC - 1:
                    mu2_done[b] = Ste.inc(te, 1, fn)
                else:
                    Ste.emit(fn)
            bank[7].acquire(Ste)
            for c in range(DC):
                fn = lambda e, c_=c: e.matmul(
                    ps[7].ap(), onesp.ap(), ch(sqt, c_),
                    start=(c_ == 0), stop=(c_ == DC - 1))
                if c == DC - 1:
                    ln2s_done[b] = Ste.inc(te, 1, fn)
                else:
                    Ste.emit(fn)

        def stage_b_tail1(b):
            Sse.wait(te, mu2_done[b])
            if b + 1 < NCH:
                Sse.wait(te, musq1_mm[b + 1])
            if b >= 1:
                Sse.wait(ve, out_done[b - 1])
            mucopy2 = Sse.inc(se, 1, lambda e: e.activation(
                out=mu_sb.ap(), in_=ps[6].ap(), func=AF.Identity))
            bank[6].release(se, mucopy2)
            sm2 = Sse.inc(se, 1, lambda e: e.activation(
                out=sqmu1.ap(), in_=mu_sb.ap()[0:1, :], func=AF.Square))
            Ste.wait(se, sm2)
            musq2_mm[b] = Ste.inc(te, 1, lambda e: e.matmul(
                ps[7].ap(), negones1.ap(), sqmu1.ap(), start=False, stop=True,
                skip_group_check=True))
            Sse.wait(te, musq2_mm[b])
            Sse.emit(lambda e: e.activation(out=lnv.ap(), in_=ps[7].ap(),
                                            func=AF.Ln, bias=epsT.ap()))
            if b + 1 < NCH:
                Sse.wait(ve, v2_done[b + 1][DC - 1])
            rs2_done[b] = Sse.inc(se, 1, lambda e: e.activation(
                out=rs_b.ap(), in_=lnv.ap(), func=AF.Exp, scale=-0.5))
            Sve.wait(se, rs2_done[b])
            if b >= 1:
                Sve.wait(te, ffn_last[b - 1])
            for c in range(DC):
                Sve.emit(lambda e, c_=c: e.tensor_tensor(
                    ps[7].ap(), ch(uv, c_), mu_sb.ap(), ALU.add))
                x1_done[b][c] = Sve.inc(ve, 1, lambda e, c_=c: e.tensor_tensor(
                    ch(x1b, c_), ps[7].ap(), rs_b.ap(), ALU.mult))
            bank[7].release(ve, x1_done[b][DC - 1])

        def stage_b_tail2(b):
            j = b % 2
            Ste.wait(dg, ffnw_ld)
            bank[6].acquire(Ste)
            bsr_done = 0
            for c in range(DC):
                Ste.wait(ve, x1_done[b][c])
                fn = lambda e, c_=c: e.matmul(
                    ps[6].ap()[0:32, :], vrb.ap()[:, c_ * 32:(c_ + 1) * 32],
                    ch(x1b, c_), start=(c_ == 0), stop=(c_ == DC - 1))
                if c == DC - 1:
                    bsr_done = Ste.inc(te, 1, fn)
                else:
                    Ste.emit(fn)
            Sse.wait(te, bsr_done)
            s12 = Sse.inc(se, 1, lambda e: e.activation(
                out=bsr_b.ap(), in_=ps[6].ap()[0:32, :], func=AF.Identity))
            bank[6].release(se, s12)
            for m in range(DC):
                bk = bank[2 + m % 2]
                bk.acquire(Ste)
                for c in range(DC):
                    Ste.emit(lambda e, c_=c, m_=m: e.matmul(
                        ps[2 + m_ % 2].ap(), fw_t(c_, m_), ch(x1b, c_),
                        start=(c_ == 0), stop=False))
                Ste.wait(se, s12)
                Ste.emit(lambda e, m_=m: e.matmul(
                    ps[2 + m_ % 2].ap(), ctb.ap()[:, m_ * 128:(m_ + 1) * 128],
                    bsr_b.ap(), start=False, stop=False))
                f_done = Ste.inc(te, 1, lambda e, m_=m: e.matmul(
                    ps[2 + m_ % 2].ap(), eyeb.ap()[:, 128:256], ch(x1b, m_),
                    start=False, stop=True))
                if m == DC - 1:
                    ffn_last[b] = f_done
                Sse.wait(te, f_done)
                Sse.wait(te, ln2s_done[b])
                Sse.wait(ve, x1_done[b][m])
                Sse.emit(lambda e, m_=m: e.activation(
                    out=ch(uv, m_), in_=ps[2 + m_ % 2].ap(), func=AF.Identity,
                    bias=ffn_bv.ap()[:, m_:m_ + 1]))
                Sse.emit(lambda e, m_=m: e.activation(
                    out=ch(sqt, m_), in_=ps[2 + m_ % 2].ap(), func=AF.Square,
                    bias=ffn_bv.ap()[:, m_:m_ + 1]))
                vev[b][m] = Sse.inc(se, 1, lambda e, m_=m: e.activation(
                    out=ch(ub, m_), in_=ps[2 + m_ % 2].ap(), func=AF.Identity,
                    bias=ffn_bv.ap()[:, m_:m_ + 1]))
                bk.release(se, vev[b][m])
            bank[6].acquire(Ste)
            for c in range(DC):
                Ste.wait(se, vev[b][c])
                fn = lambda e, c_=c: e.matmul(
                    ps[6].ap(), onesn.ap(), ch(ub, c_),
                    start=(c_ == 0), stop=(c_ == DC - 1))
                if c == DC - 1:
                    mu3 = Ste.inc(te, 1, fn)
                else:
                    Ste.emit(fn)
            bank[7].acquire(Ste)
            for c in range(DC):
                fn = lambda e, c_=c: e.matmul(
                    ps[7].ap(), onesp.ap(), ch(sqt, c_),
                    start=(c_ == 0), stop=(c_ == DC - 1))
                if c == DC - 1:
                    ln3s_done[b] = Ste.inc(te, 1, fn)
                else:
                    Ste.emit(fn)
            Sse.wait(te, mu3)
            Sse.wait(te, musq2_mm[b])
            Sse.wait(ve, x1_done[b][DC - 1])
            mucopy3 = Sse.inc(se, 1, lambda e: e.activation(
                out=mu_sb.ap(), in_=ps[6].ap(), func=AF.Identity))
            bank[6].release(se, mucopy3)
            sm3 = Sse.inc(se, 1, lambda e: e.activation(
                out=sqmu1.ap(), in_=mu_sb.ap()[0:1, :], func=AF.Square))
            Ste.wait(se, sm3)
            musq3_mm[b] = Ste.inc(te, 1, lambda e: e.matmul(
                ps[7].ap(), negones1.ap(), sqmu1.ap(), start=False, stop=True,
                skip_group_check=True))
            Sse.wait(te, musq3_mm[b])
            Sse.emit(lambda e: e.activation(out=lnv.ap(), in_=ps[7].ap(),
                                            func=AF.Ln, bias=epsT.ap()))
            Sse.wait(ve, x1_done[b][DC - 1])
            rs3_done[b] = Sse.inc(se, 1, lambda e: e.activation(
                out=rs_b.ap(), in_=lnv.ap(), func=AF.Exp, scale=-0.5))
            Sve.wait(se, rs3_done[b])
            for c in range(DC):
                Sve.emit(lambda e, c_=c: e.tensor_tensor(
                    ps[7].ap(), ch(uv, c_), mu_sb.ap(), ALU.add))
                fn = lambda e, c_=c: e.tensor_tensor(
                    ch(uv, c_), ps[7].ap(), rs_b.ap(), ALU.mult)
                if c == DC - 1:
                    out_done[b] = Sve.inc(ve, 1, fn)
                else:
                    Sve.emit(fn)
            bank[7].release(ve, out_done[b])
            Ssy.wait(ve, out_done[b])
            t0 = b * C
            Ssy.inc(do, 16, lambda e, t0_=t0: e.dma_start(
                out=y_d.rearrange("(c p) t -> p c t", p=128)[:, :, t0_:t0_ + C],
                in_=uv.ap().rearrange("p (c t) -> p c t", c=DC)))
            if b + 2 < NCH:
                Sgp.wait(te, op_done[b][DC - 1])
                emit_x0_load(b + 2)
                emit_s1(b + 2)

        a1_done = [0] * NCH
        mu2_done = [0] * NCH
        ct_done = [0] * NCH
        dec_done = [0] * NCH
        y_done = [0] * NCH
        tanh_done = [[0] * DC for _ in range(NCH)]
        for i in range(NCH + 1):
            b = i - 1
            if i < NCH:
                stage_a_stats1(i)
                stage_a_stats2(i)
            if b >= 0:
                stage_b_g(b)
                stage_b_gates_nle(b)
            if i < NCH:
                stage_a_h(i, 0, 1, conv_b=b if b >= 0 else None)
                if b >= 0:
                    stage_b_gates_exp(b)
                stage_a_h(i, 1, DC, conv_b=b if b >= 0 else None)
            elif b >= 0:
                stage_b_gates_exp(b)
                for c in range(DC):
                    emit_conv_te(b, c)
            if b >= 0:
                stage_b_op(b)
                stage_b_tail1(b)
            if i < NCH:
                stage_a_h(i, DC, 16)
            if b >= 0:
                stage_b_tail2(b)

        final = do.n
        Ssy.emit(lambda e, v=final: e.wait_ge(do.h, v))

        block.sync(Ssy.replay)
        block.gpsimd(Sgp.replay)
        block.scalar(Sse.replay)
        block.vector(Sve.replay)
        block.tensor(Ste.replay)

    return nc


_NC = None


def _conv_diag(conv_w):
    out = np.zeros((128, 3 * DC * 128), dtype=np.float32)
    idx = np.arange(128)
    for c in range(DC):
        for k in range(3):
            out[idx, (c * 3 + k) * 128 + idx] = conv_w[c * 128 + idx, 0, k]
    return out


def kernel(L0_attn_x, L0_attn_norm_w, L0_attn_norm_b, L0_attn_in_w, L0_attn_in_b,
           L0_attn_conv_w, L0_attn_conv_b, L0_attn_dt_w, L0_attn_dt_b,
           L0_attn_b_w, L0_attn_b_b, L0_attn_c_w, L0_attn_c_b, L0_attn_a_log,
           L0_attn_d, L0_attn_out_w, L0_attn_out_b, L0_ffn_W, L0_ffn_b,
           L0_ffn_V_r, L0_ffn_C):
    global _NC
    if _NC is None:
        _NC = build_program()
    nc = _NC

    f32 = np.float32
    asc = np.ascontiguousarray
    x = np.asarray(L0_attn_x, dtype=f32)

    shared = {
        "in_wT": asc(np.asarray(L0_attn_in_w, f32).T),
        "g_wT": asc(np.concatenate([np.asarray(L0_attn_dt_w, f32),
                                    np.asarray(L0_attn_b_w, f32),
                                    np.asarray(L0_attn_c_w, f32)], 0).T),
        "out_wT": asc(np.asarray(L0_attn_out_w, f32).T),
        "ffn_wT": asc(np.asarray(L0_ffn_W, f32).T),
        "v_r": asc(np.asarray(L0_ffn_V_r, f32)),
        "c_T": asc(np.asarray(L0_ffn_C, f32).T),
        "eye2": asc(np.concatenate([2.0 * np.eye(128, dtype=f32),
                                    np.eye(128, dtype=f32)], 1)),
        "in_bv_hbm": asc(np.asarray(L0_attn_in_b, f32).reshape(16, 128).T),
        "norm_wv_hbm": asc(np.asarray(L0_attn_norm_w, f32).reshape(DC, 128).T),
        "norm_bv_hbm": asc(np.asarray(L0_attn_norm_b, f32).reshape(DC, 128).T),
        "out_bv_hbm": asc(np.asarray(L0_attn_out_b, f32).reshape(DC, 128).T),
        "ffn_bv_hbm": asc(np.asarray(L0_ffn_b, f32).reshape(DC, 128).T),
        "g_bv_hbm": asc(np.stack([np.asarray(L0_attn_dt_b, f32),
                              np.asarray(L0_attn_b_b, f32),
                              np.asarray(L0_attn_c_b, f32)], 1)),
        "conv_wd_hbm": _conv_diag(np.asarray(L0_attn_conv_w, f32)),
        "conv_bv_hbm": asc(np.asarray(L0_attn_conv_b, f32).reshape(DC, 128).T),
        "d_v_hbm": asc(np.asarray(L0_attn_d, f32).reshape(DC, 128).T),
        "alog_v_hbm": asc(np.asarray(L0_attn_a_log, f32).reshape(S, 1)),
    }

    in_maps = []
    for core in range(NCORES):
        m = dict(shared)
        m["x"] = asc(x[core * BL:(core + 1) * BL].reshape(TOK, D).T)
        in_maps.append(m)

    res = run_bass_kernel_spmd(nc, in_maps, core_ids=list(range(NCORES)))
    outs = []
    for core in range(NCORES):
        yc = np.asarray(res.results[core]["y"])
        outs.append(yc.T.reshape(BL, T, D))
    return np.concatenate(outs, 0).astype(np.float32)


# version 14
# speedup vs baseline: 1.6321x; 1.0227x over previous
import numpy as np
import concourse.bass as bass
import concourse.mybir as mybir
from concourse.bass_utils import run_bass_kernel_spmd

F32 = mybir.dt.float32
BF16 = mybir.dt.bfloat16
AF = mybir.ActivationFunctionType
ALU = mybir.AluOpType

NCORES = 8
B, T, D, S = 16, 2048, 1024, 64
BL = B // NCORES
TOK = BL * T
C = 512
NCH = TOK // C
CPS = T // C
DC = D // 128
EPS = 1e-5


class Sem:
    def __init__(self, h):
        self.h = h
        self.n = 0


class Stream:
    def __init__(self, name):
        self.name = name
        self.ops = []
        self.hw = {}

    def emit(self, fn):
        self.ops.append(fn)

    def wait(self, sem, val):
        if not val or val <= self.hw.get(id(sem), 0):
            return
        self.hw[id(sem)] = val
        self.ops.append(lambda e, s=sem.h, v=val: e.wait_ge(s, v))

    def inc(self, sem, k, fn):
        sem.n += k
        self.ops.append(lambda e, f=fn, s=sem.h, kk=k: f(e).then_inc(s, kk))
        return sem.n

    def replay(self, e):
        for f in self.ops:
            f(e)


class Bank:
    def __init__(self, idx):
        self.idx = idx
        self.rel = []

    def acquire(self, stream):
        for sem, val in self.rel:
            stream.wait(sem, val)
        self.rel = []

    def release(self, sem, val):
        self.rel.append((sem, val))


def build_program():
    nc = bass.Bass(trn_type="TRN2", target_bir_lowering=False,
                   detect_race_conditions=False)

    x_d = nc.dram_tensor("x", [D, TOK], F32, kind="ExternalInput").ap()
    y_d = nc.dram_tensor("y", [D, TOK], F32, kind="ExternalOutput").ap()
    inw_d = nc.dram_tensor("in_wT", [D, 2 * D], F32, kind="ExternalInput").ap()
    gw_d = nc.dram_tensor("g_wT", [D, 3 * S], F32, kind="ExternalInput").ap()
    ow_d = nc.dram_tensor("out_wT", [D + S, D], F32, kind="ExternalInput").ap()
    fw_d = nc.dram_tensor("ffn_wT", [D, D], F32, kind="ExternalInput").ap()
    vr_d = nc.dram_tensor("v_r", [D, 32], F32, kind="ExternalInput").ap()
    ct_d = nc.dram_tensor("c_T", [32, D], F32, kind="ExternalInput").ap()
    eye_d = nc.dram_tensor("eye2", [128, 256], F32, kind="ExternalInput").ap()
    inb_d = nc.dram_tensor("in_bv_hbm", [128, 16], F32, kind="ExternalInput").ap()
    nw_d = nc.dram_tensor("norm_wv_hbm", [128, DC], F32, kind="ExternalInput").ap()
    nb_d = nc.dram_tensor("norm_bv_hbm", [128, DC], F32, kind="ExternalInput").ap()
    ob_d = nc.dram_tensor("out_bv_hbm", [128, DC], F32, kind="ExternalInput").ap()
    fb_d = nc.dram_tensor("ffn_bv_hbm", [128, DC], F32, kind="ExternalInput").ap()
    gb_d = nc.dram_tensor("g_bv_hbm", [S, 3], F32, kind="ExternalInput").ap()
    cw_d = nc.dram_tensor("conv_wd_hbm", [128, 3 * DC * 128], F32, kind="ExternalInput").ap()
    cb_d = nc.dram_tensor("conv_bv_hbm", [128, DC], F32, kind="ExternalInput").ap()
    dv_d = nc.dram_tensor("d_v_hbm", [128, DC], F32, kind="ExternalInput").ap()
    al_d = nc.dram_tensor("alog_v_hbm", [S, 1], F32, kind="ExternalInput").ap()

    x_r = x_d.rearrange("(c p) t -> c p t", p=128)
    y_r = y_d.rearrange("(c p) t -> c p t", p=128)

    A = nc.alloc_sbuf_tensor
    in_wb = A("in_wb", [128, DC * 2 * D], BF16)
    g_wb = A("g_wb", [128, DC * 3 * S], BF16)
    out_wb = A("out_wb", [128, DC * D], BF16)
    out_wyb = A("out_wyb", [S, D], BF16)
    ffn_wb = A("ffn_wb", [128, DC * D], BF16)
    vrb = A("vrb", [128, DC * 32], BF16)
    ctb = A("ctb", [32, D], BF16)
    eyeb = A("eyeb", [128, 256], BF16)
    onesn = A("onesn", [128, 128], BF16)
    onesp = A("onesp", [128, 128], BF16)

    in_bv = A("in_bv", [128, 16], F32)
    norm_wv = A("norm_wv", [128, DC], F32)
    norm_bv = A("norm_bv", [128, DC], F32)
    out_bv = A("out_bv", [128, DC], F32)
    ffn_bv = A("ffn_bv", [128, DC], F32)
    g_bv = A("g_bv", [S, 3], F32)
    conv_bv = A("conv_bv", [128, DC], F32)
    d_v = A("d_v", [128, DC], F32)
    alog_v = A("alog_v", [S, 1], F32)
    tmpa = A("tmpa", [S, 1], F32)
    aT = A("aT", [S, 1], F32)
    a1e4T = A("a1e4T", [S, 1], F32)
    epsT = A("epsT", [128, 1], F32)

    x0b = A("x0b", [128, 2 * DC * C], BF16)
    sqt = A("sqt", [128, DC * C], BF16)
    tt = A("tt", [128, DC * C], BF16)
    nbb = A("nbb", [128, DC], BF16)
    in_bv2 = A("in_bv2", [128, 16], F32)
    xp = A("xp", [128, 2 * DC * 514], BF16)
    zt = A("zt", [128, 2 * DC * C], BF16)
    xcr = A("xcr", [128, 2 * C], BF16)
    xconv = A("xconv", [128, DC * C], BF16)
    cwd = A("cwd", [128, 3 * DC * 128], BF16)
    gtA = A("gtA", [S, C], F32)
    gtB = A("gtB", [S, C], F32)
    gtC = A("gtC", [S, C], F32)
    bT = A("bT", [S, C], F32)
    cTt = A("cTt", [S, C], F32)
    states = A("states", [S, C], F32)
    carry = A("carry", [S, 1], F32)
    yB = A("yB", [S, C], BF16)
    uv = A("uv", [128, DC * C], F32)
    ub = A("ub", [128, DC * C], BF16)
    x1b = A("x1b", [128, DC * C], BF16)
    bsr_b = A("bsr_b", [32, C], BF16)
    sqmu1 = A("sqmu1", [1, C], BF16)
    negones1 = A("negones1", [1, 128], BF16)
    mu_sb = A("mu_sb", [128, C], F32)
    lnv = A("lnv", [128, C], F32)
    rs_b = A("rs_b", [128, C], F32)

    ps = [nc.alloc_psum_tensor(f"ps{i}", [128, C], F32) for i in range(8)]
    bank = [Bank(i) for i in range(8)]

    def inw_t(c, m):
        return in_wb.ap()[:, c * 2 * D + m * 128: c * 2 * D + (m + 1) * 128]

    def gw_t(c, g):
        return g_wb.ap()[:, c * 3 * S + g * S: c * 3 * S + (g + 1) * S]

    def ow_t(c, m):
        return out_wb.ap()[:, c * D + m * 128: c * D + (m + 1) * 128]

    def fw_t(c, m):
        return ffn_wb.ap()[:, c * D + m * 128: c * D + (m + 1) * 128]

    def x0_t(j, c):
        return x0b.ap()[:, (j * DC + c) * C: (j * DC + c + 1) * C]

    def ch(tile, c):
        return tile.ap()[:, c * C: (c + 1) * C]

    def xp_t(j, c, lo, hi):
        base = (j * DC + c) * 514
        return xp.ap()[:, base + lo: base + hi]

    def z_t(j, c):
        return zt.ap()[:, (j * DC + c) * C: (j * DC + c + 1) * C]

    def cwd_t(c, k):
        return cwd.ap()[:, (c * 3 + k) * 128:(c * 3 + k + 1) * 128]

    def xcr_t(c):
        return xcr.ap()[:, (c % 2) * C:(c % 2 + 1) * C]

    with (
        nc.semaphore("dma_w") as h_dw,
        nc.semaphore("dma_g") as h_dg,
        nc.semaphore("dma_o") as h_do,
        nc.semaphore("s_se") as h_se,
        nc.semaphore("s_ve") as h_ve,
        nc.semaphore("s_te") as h_te,
        nc.Block() as block,
    ):
        dw, dg, do = Sem(h_dw), Sem(h_dg), Sem(h_do)
        se, ve, te = Sem(h_se), Sem(h_ve), Sem(h_te)
        Ssy, Sgp, Sse, Sve, Ste = (Stream("sync"), Stream("gpsimd"),
                                   Stream("scalar"), Stream("vector"),
                                   Stream("tensor"))

        x0_ld_list = [0] * NCH

        def emit_x0_load(i):
            t0 = i * C
            x0_ld_list[i] = Sgp.inc(dg, 16, lambda e, j=i % 2, t0_=t0: e.dma_start(
                out=x0b.ap()[:, j * DC * C:(j + 1) * DC * C]
                    .rearrange("p (c t) -> p c t", c=DC),
                in_=x_d.rearrange("(c p) t -> p c t", p=128)[:, :, t0_: t0_ + C]))

        a2_done = [0] * NCH
        musq1_mm = [0] * NCH
        musq2_mm = [0] * NCH
        musq3_mm = [0] * NCH
        rsA_done = [0] * NCH
        v2_done = [[0] * DC for _ in range(NCH)]
        t_done = [[0] * DC for _ in range(NCH)]
        h_done = [[0] * 16 for _ in range(NCH)]
        hevac = [[0] * 16 for _ in range(NCH)]
        edge_r = [[0] * DC for _ in range(NCH)]
        gproj_done = [0] * NCH
        conv_done = [[0] * DC for _ in range(NCH)]
        prev_tanh = [0] * NCH
        xcd_done = [[0] * DC for _ in range(NCH)]
        op_done = [[0] * DC for _ in range(NCH)]
        uev = [[0] * DC for _ in range(NCH)]
        ln2s_done = [0] * NCH
        rs2_done = [0] * NCH
        x1_done = [[0] * DC for _ in range(NCH)]
        ffn_last = [0] * NCH
        vev = [[0] * DC for _ in range(NCH)]
        ln3s_done = [0] * NCH
        rs3_done = [0] * NCH
        out_done = [0] * NCH
        x0_ld = [0] * NCH

        for dram, sb in [(inb_d, in_bv), (nw_d, norm_wv), (nb_d, norm_bv),
                         (ob_d, out_bv), (fb_d, ffn_bv), (gb_d, g_bv),
                         (cb_d, conv_bv), (dv_d, d_v),
                         (al_d, alog_v)]:
            Ssy.inc(dw, 16, lambda e, d_=dram, s_=sb: e.dma_start(out=s_.ap(), in_=d_))
        n_vec_loads = dw.n

        Sgp.inc(dg, 16, lambda e: e.dma_start(out=eyeb.ap(), in_=eye_d))
        inw_ld = Sgp.inc(dg, 16, lambda e: e.dma_start(
            out=in_wb.ap().rearrange("p (c m) -> p c m", c=DC),
            in_=inw_d.rearrange("(c p) m -> p c m", p=128)))
        emit_x0_load(0)
        emit_x0_load(1)
        cwd_ld = Sgp.inc(dg, 16, lambda e: e.dma_start(out=cwd.ap(), in_=cw_d))
        gw_ld = Sgp.inc(dg, 16, lambda e: e.dma_start(
            out=g_wb.ap().rearrange("p (c m) -> p c m", c=DC),
            in_=gw_d.rearrange("(c p) m -> p c m", p=128)))
        Sgp.inc(dg, 16, lambda e: e.dma_start(
            out=out_wb.ap().rearrange("p (c m) -> p c m", c=DC),
            in_=ow_d[0:D, :].rearrange("(c p) m -> p c m", p=128)))
        ow_ld = Sgp.inc(dg, 16, lambda e: e.dma_start(
            out=out_wyb.ap(), in_=ow_d[D:D + S, :]))
        Sgp.inc(dg, 16, lambda e: e.dma_start(
            out=ffn_wb.ap().rearrange("p (c m) -> p c m", c=DC),
            in_=fw_d.rearrange("(c p) m -> p c m", p=128)))
        Sgp.inc(dg, 16, lambda e: e.dma_start(
            out=vrb.ap().rearrange("p (c m) -> p c m", c=DC),
            in_=vr_d.rearrange("(c p) m -> p c m", p=128)))
        ffnw_ld = Sgp.inc(dg, 16, lambda e: e.dma_start(out=ctb.ap(), in_=ct_d))

        Sve.emit(lambda e: e.memset(onesn.ap(), -1.0 / D))
        Sve.emit(lambda e: e.memset(negones1.ap(), -1.0))
        Sve.emit(lambda e: e.memset(onesp.ap(), 1.0 / D))
        memset_done = Sve.inc(ve, 1, lambda e: e.memset(epsT.ap(), EPS))
        Sse.wait(dw, n_vec_loads)
        a_exp = Sse.inc(se, 1, lambda e: e.activation(
            out=tmpa.ap(), in_=alog_v.ap(), func=AF.Exp))
        Sve.wait(se, a_exp)
        Sve.emit(lambda e: e.tensor_scalar_mul(aT.ap(), tmpa.ap(), -1.0))
        a_prep = Sve.inc(ve, 1, lambda e: e.tensor_scalar_mul(
            a1e4T.ap(), aT.ap(), 1e-4))
        nbb_done = Sse.inc(se, 1, lambda e: e.activation(
            out=nbb.ap(), in_=norm_bv.ap(), func=AF.Copy))
        Ste.wait(dg, inw_ld)
        Ste.wait(se, nbb_done)
        mv_evac = [0, 0]
        for m in range(16):
            Ste.wait(se, mv_evac[m % 2])
            for c in range(DC):
                fn = lambda e, c_=c, m_=m: e.matmul(
                    ps[m_ % 2].ap()[:, 0:1], inw_t(c_, m_),
                    nbb.ap()[:, c_:c_ + 1],
                    start=(c_ == 0), stop=(c_ == DC - 1))
                if c == DC - 1:
                    mv = Ste.inc(te, 1, fn)
                else:
                    Ste.emit(fn)
            Sse.wait(te, mv)
            mv_evac[m % 2] = Sse.inc(se, 1, lambda e, m_=m: e.activation(
                out=in_bv2.ap()[:, m_:m_ + 1], in_=ps[m_ % 2].ap()[:, 0:1],
                func=AF.Identity, bias=in_bv.ap()[:, m_:m_ + 1]))
        bank[0].release(se, mv_evac[0])
        bank[1].release(se, mv_evac[1])

        sq_ready = [0] * NCH

        def emit_s1(i):
            j = i % 2
            Sve.wait(dg, x0_ld_list[i])
            if i >= 2:
                Sve.wait(te, ln3s_done[i - 2])
            for c in range(DC):
                fn = lambda e, c_=c: e.tensor_tensor(
                    ch(sqt, c_), x0_t(j, c_), x0_t(j, c_), ALU.mult)
                if c == DC - 1:
                    sq_ready[i] = Sve.inc(ve, 1, fn)
                else:
                    Sve.emit(fn)

        def stage_a_stats1(i):
            j = i % 2
            if not sq_ready[i]:
                emit_s1(i)
            Ste.wait(dg, x0_ld_list[i])
            Ste.wait(ve, memset_done)
            bank[4].acquire(Ste)
            for c in range(DC):
                fn = lambda e, c_=c: e.matmul(
                    ps[4].ap(), onesn.ap(), x0_t(j, c_),
                    start=(c_ == 0), stop=(c_ == DC - 1))
                if c == DC - 1:
                    a1_done[i] = Ste.inc(te, 1, fn)
                else:
                    Ste.emit(fn)

        def stage_a_stats2(i):
            li = i % CPS
            j = i % 2
            Ste.wait(ve, sq_ready[i])
            bank[5].acquire(Ste)
            for c in range(DC):
                fn = lambda e, c_=c: e.matmul(
                    ps[5].ap(), onesp.ap(), ch(sqt, c_),
                    start=(c_ == 0), stop=(c_ == DC - 1))
                if c == DC - 1:
                    a2_done[i] = Ste.inc(te, 1, fn)
                else:
                    Ste.emit(fn)
            Sse.wait(te, a1_done[i])
            if i >= 2:
                Sse.wait(te, musq3_mm[i - 2])
            s2 = Sse.inc(se, 1, lambda e: e.activation(
                out=sqmu1.ap(), in_=ps[4].ap()[0:1, :], func=AF.Square))
            Ste.wait(se, s2)
            musq1_mm[i] = Ste.inc(te, 1, lambda e: e.matmul(
                ps[5].ap(), negones1.ap(), sqmu1.ap(), start=False, stop=True,
                skip_group_check=True))
            Sse.wait(te, musq1_mm[i])
            Sse.emit(lambda e: e.activation(out=lnv.ap(), in_=ps[5].ap(),
                                            func=AF.Ln, bias=epsT.ap()))
            if i >= 2:
                Sse.wait(ve, out_done[i - 2])
            rsA_done[i] = Sse.inc(se, 1, lambda e: e.activation(
                out=rs_b.ap(), in_=lnv.ap(), func=AF.Exp, scale=-0.5))
            Sve.wait(se, rsA_done[i])
            if i >= 1:
                Sve.wait(te, h_done[i - 1][15])
            for c in range(DC):
                Sve.emit(lambda e, c_=c: e.tensor_tensor(
                    ps[5].ap(), x0_t(j, c_), ps[4].ap(), ALU.add))
                v2_done[i][c] = Sve.inc(ve, 1, lambda e, c_=c: e.scalar_tensor_tensor(
                    ch(tt, c_), ps[5].ap(), norm_wv.ap()[:, c_:c_ + 1],
                    rs_b.ap(), ALU.mult, ALU.mult))
                t_done[i][c] = v2_done[i][c]
            bank[4].release(ve, v2_done[i][DC - 1])
            bank[5].release(ve, v2_done[i][DC - 1])
            if i >= 2:
                Sse.wait(te, conv_done[i - 2][DC - 1])
            if li == 0:
                for m in range(DC):
                    Sse.emit(lambda e, m_=m: e.activation(
                        out=xp_t(j, m_, 0, 1), in_=epsT.ap(),
                        func=AF.Identity, scale=0.0))
            else:
                for m in range(DC):
                    Sse.emit(lambda e, m_=m: e.activation(
                        out=xp_t(j, m_, 0, 1),
                        in_=xp_t(1 - j, m_, 512, 513), func=AF.Identity))

        def stage_a_h(i, m_lo, m_hi):
            li = i % CPS
            j = i % 2
            Ste.wait(dg, inw_ld)
            for m in range(m_lo, m_hi):
                bk = bank[m % 2]
                bk.acquire(Ste)
                for c in range(DC):
                    Ste.wait(ve, t_done[i][c])
                    fn = lambda e, c_=c, m_=m: e.matmul(
                        ps[m_ % 2].ap(), inw_t(c_, m_), ch(tt, c_),
                        start=(c_ == 0), stop=(c_ == DC - 1))
                    if c == DC - 1:
                        h_done[i][m] = Ste.inc(te, 1, fn)
                    else:
                        Ste.emit(fn)
                Sse.wait(te, h_done[i][m])
                if m < DC:
                    if i >= 1 and li != 0:
                        Sse.emit(lambda e, m_=m: e.activation(
                            out=xp_t(j, m_, 1, 513), in_=ps[m_ % 2].ap(),
                            func=AF.Identity, bias=in_bv2.ap()[:, m_:m_ + 1]))
                        hevac[i][m] = Sse.inc(se, 1, lambda e, m_=m: e.activation(
                            out=xp_t(1 - j, m_, 513, 514),
                            in_=ps[m_ % 2].ap()[:, 0:1],
                            func=AF.Identity, bias=in_bv2.ap()[:, m_:m_ + 1]))
                        edge_r[i - 1][m] = hevac[i][m]
                    else:
                        hevac[i][m] = Sse.inc(se, 1, lambda e, m_=m: e.activation(
                            out=xp_t(j, m_, 1, 513), in_=ps[m_ % 2].ap(),
                            func=AF.Identity, bias=in_bv2.ap()[:, m_:m_ + 1]))
                else:
                    if i >= 2:
                        Sse.wait(te, gproj_done[i - 2])
                    hevac[i][m] = Sse.inc(se, 1, lambda e, m_=m: e.activation(
                        out=z_t(j, m_ - DC), in_=ps[m_ % 2].ap(),
                        func=AF.Identity, bias=in_bv2.ap()[:, m_:m_ + 1]))
                bk.release(se, hevac[i][m])

        def emit_conv_te(b, c):
            jb = b % 2
            bki = 2 + c % 2
            Ste.wait(dg, cwd_ld)
            Ste.wait(se, edge_r[b][c])
            Ste.wait(se, hevac[b][c])
            bank[bki].acquire(Ste)
            Ste.emit(lambda e, c_=c, k_=bki: e.matmul(
                ps[k_].ap(), cwd_t(c_, 0), xp_t(jb, c_, 0, 512),
                start=True, stop=False))
            Ste.emit(lambda e, c_=c, k_=bki: e.matmul(
                ps[k_].ap(), cwd_t(c_, 1), xp_t(jb, c_, 1, 513),
                start=False, stop=False))
            conv_done[b][c] = Ste.inc(te, 1, lambda e, c_=c, k_=bki: e.matmul(
                ps[k_].ap(), cwd_t(c_, 2), xp_t(jb, c_, 2, 514),
                start=False, stop=True))
            Sse.wait(te, conv_done[b][c])
            if c >= 2:
                Sse.wait(ve, xcd_done[b][c - 2])
            tanh_done[b][c] = Sse.inc(se, 1, lambda e, c_=c, k_=bki: e.activation(
                out=xcr_t(c_), in_=ps[k_].ap(),
                func=AF.Tanh, bias=conv_bv.ap()[:, c_:c_ + 1]))
            bank[bki].release(se, tanh_done[b][c])
            Sve.wait(se, tanh_done[b][c])
            if c == 0 and b >= 1:
                Sve.wait(te, op_done[b - 1][DC - 1])
            xcd_done[b][c] = Sve.inc(ve, 1, lambda e, c_=c: e.tensor_scalar_mul(
                ch(xconv, c_), xcr_t(c_), d_v.ap()[:, c_:c_ + 1]))

        def stage_b_g(b):
            li = b % CPS
            j = b % 2
            if li == CPS - 1:
                ze = 0
                for m in range(DC):
                    fn = lambda e, m_=m: e.activation(
                        out=xp_t(j, m_, 513, 514), in_=epsT.ap(),
                        func=AF.Identity, scale=0.0)
                    if m == DC - 1:
                        ze = Sse.inc(se, 1, fn)
                    else:
                        Sse.emit(fn)
                for m in range(DC):
                    edge_r[b][m] = ze
            Ste.wait(dg, gw_ld)
            Ste.wait(se, hevac[b][15])
            gd = 0
            for g, bki in [(0, 2), (1, 3), (2, 6)]:
                bank[bki].acquire(Ste)
                for c in range(DC):
                    fn = lambda e, c_=c, g_=g, k_=bki: e.matmul(
                        ps[k_].ap()[0:S, :], gw_t(c_, g_), z_t(j, c_),
                        start=(c_ == 0), stop=(c_ == DC - 1))
                    if c == DC - 1:
                        gd = Ste.inc(te, 1, fn)
                    else:
                        Ste.emit(fn)
            gproj_done[b] = gd

        def stage_b_gates_nle(b):
            Sse.wait(te, gproj_done[b])
            if b >= 1:
                Sse.wait(ve, y_done[b - 1])
            Sse.emit(lambda e: e.activation(
                out=gtA.ap(), in_=ps[2].ap()[0:S, :], func=AF.Abs,
                bias=g_bv.ap()[:, 0:1]))
            g_rel = Sse.inc(se, 1, lambda e: e.activation(
                out=gtC.ap(), in_=ps[2].ap()[0:S, :], func=AF.Relu,
                bias=g_bv.ap()[:, 0:1]))
            bank[2].release(se, g_rel)
            Sse.emit(lambda e: e.activation(
                out=gtB.ap(), in_=gtA.ap(), func=AF.Exp, scale=-1.0))
            gln = Sse.inc(se, 1, lambda e: e.activation(
                out=gtA.ap(), in_=gtB.ap(), func=AF.Ln, bias=1.0))
            Sve.wait(se, gln)
            spt = Sve.inc(ve, 1, lambda e: e.tensor_tensor(
                gtB.ap(), gtC.ap(), gtA.ap(), ALU.add))
            Sse.wait(ve, spt)
            Sse.wait(ve, a_prep)
            dec_done[b] = Sse.inc(se, 1, lambda e: e.activation(
                out=gtC.ap(), in_=gtB.ap(), func=AF.Exp,
                scale=aT.ap(), bias=a1e4T.ap()))

        def stage_b_gates_exp(b):
            li = b % CPS
            bt_done = Sse.inc(se, 1, lambda e: e.activation(
                out=bT.ap(), in_=ps[3].ap()[0:S, :], func=AF.Tanh,
                bias=g_bv.ap()[:, 1:2]))
            bank[3].release(se, bt_done)
            ct_done[b] = Sse.inc(se, 1, lambda e: e.activation(
                out=cTt.ap(), in_=ps[6].ap()[0:S, :], func=AF.Tanh,
                bias=g_bv.ap()[:, 2:3]))
            bank[6].release(se, ct_done[b])
            Sve.wait(se, dec_done[b])
            Sve.wait(se, bt_done)
            ini = 0.0 if li == 0 else carry.ap()
            scan_done = Sve.inc(ve, 1, lambda e, ini_=ini: e.tensor_tensor_scan(
                out=states.ap(), data0=gtC.ap(), data1=bT.ap(),
                initial=ini_, op0=ALU.mult, op1=ALU.add))
            Sve.wait(ve, scan_done)
            Sve.emit(lambda e: e.tensor_copy(carry.ap(), states.ap()[:, C - 1:C]))
            Sve.wait(se, ct_done[b])
            if b >= 1:
                Sve.wait(te, op_done[b - 1][DC - 1])
            y_done[b] = Sve.inc(ve, 1, lambda e: e.tensor_tensor(
                yB.ap(), cTt.ap(), states.ap(), ALU.mult))

        def stage_b_op(b):
            j = b % 2
            Ste.wait(dg, ow_ld)
            for m in range(DC):
                bk = bank[2 + m % 2]
                bk.acquire(Ste)
                for c in range(DC):
                    Ste.wait(ve, xcd_done[b][c])
                    Ste.emit(lambda e, c_=c, m_=m: e.matmul(
                        ps[2 + m_ % 2].ap(), ow_t(c_, m_), ch(xconv, c_),
                        start=(c_ == 0), stop=False))
                Ste.wait(ve, y_done[b])
                Ste.emit(lambda e, m_=m: e.matmul(
                    ps[2 + m_ % 2].ap(), out_wyb.ap()[:, m_ * 128:(m_ + 1) * 128],
                    yB.ap(), start=False, stop=False))
                op_done[b][m] = Ste.inc(te, 1, lambda e, m_=m: e.matmul(
                    ps[2 + m_ % 2].ap(), eyeb.ap()[:, 0:128], x0_t(j, m_),
                    start=False, stop=True))
                Sse.wait(te, op_done[b][m])
                if m == 0:
                    if b >= 1:
                        Sse.wait(do, 16 * b)
                        Sse.wait(te, ln3s_done[b - 1])
                    Sse.wait(te, a2_done[b + 1] if b + 1 < NCH else a2_done[b])
                Sse.emit(lambda e, m_=m: e.activation(
                    out=ch(uv, m_), in_=ps[2 + m_ % 2].ap(), func=AF.Identity,
                    bias=out_bv.ap()[:, m_:m_ + 1]))
                Sse.emit(lambda e, m_=m: e.activation(
                    out=ch(sqt, m_), in_=ps[2 + m_ % 2].ap(), func=AF.Square,
                    bias=out_bv.ap()[:, m_:m_ + 1]))
                uev[b][m] = Sse.inc(se, 1, lambda e, m_=m: e.activation(
                    out=ch(ub, m_), in_=ps[2 + m_ % 2].ap(), func=AF.Identity,
                    bias=out_bv.ap()[:, m_:m_ + 1]))
                bk.release(se, uev[b][m])
            bank[6].acquire(Ste)
            for c in range(DC):
                Ste.wait(se, uev[b][c])
                fn = lambda e, c_=c: e.matmul(
                    ps[6].ap(), onesn.ap(), ch(ub, c_),
                    start=(c_ == 0), stop=(c_ == DC - 1))
                if c == DC - 1:
                    mu2_done[b] = Ste.inc(te, 1, fn)
                else:
                    Ste.emit(fn)
            bank[7].acquire(Ste)
            for c in range(DC):
                fn = lambda e, c_=c: e.matmul(
                    ps[7].ap(), onesp.ap(), ch(sqt, c_),
                    start=(c_ == 0), stop=(c_ == DC - 1))
                if c == DC - 1:
                    ln2s_done[b] = Ste.inc(te, 1, fn)
                else:
                    Ste.emit(fn)

        def stage_b_tail1(b):
            Sse.wait(te, mu2_done[b])
            if b + 1 < NCH:
                Sse.wait(te, musq1_mm[b + 1])
            if b >= 1:
                Sse.wait(ve, out_done[b - 1])
            mucopy2 = Sse.inc(se, 1, lambda e: e.activation(
                out=mu_sb.ap(), in_=ps[6].ap(), func=AF.Identity))
            bank[6].release(se, mucopy2)
            sm2 = Sse.inc(se, 1, lambda e: e.activation(
                out=sqmu1.ap(), in_=mu_sb.ap()[0:1, :], func=AF.Square))
            Ste.wait(se, sm2)
            musq2_mm[b] = Ste.inc(te, 1, lambda e: e.matmul(
                ps[7].ap(), negones1.ap(), sqmu1.ap(), start=False, stop=True,
                skip_group_check=True))
            Sse.wait(te, musq2_mm[b])
            Sse.emit(lambda e: e.activation(out=lnv.ap(), in_=ps[7].ap(),
                                            func=AF.Ln, bias=epsT.ap()))
            if b + 1 < NCH:
                Sse.wait(ve, v2_done[b + 1][DC - 1])
            rs2_done[b] = Sse.inc(se, 1, lambda e: e.activation(
                out=rs_b.ap(), in_=lnv.ap(), func=AF.Exp, scale=-0.5))
            Sve.wait(se, rs2_done[b])
            if b >= 1:
                Sve.wait(te, ffn_last[b - 1])
            for c in range(DC):
                Sve.emit(lambda e, c_=c: e.tensor_tensor(
                    ps[7].ap(), ch(uv, c_), mu_sb.ap(), ALU.add))
                x1_done[b][c] = Sve.inc(ve, 1, lambda e, c_=c: e.tensor_tensor(
                    ch(x1b, c_), ps[7].ap(), rs_b.ap(), ALU.mult))
            bank[7].release(ve, x1_done[b][DC - 1])

        def stage_b_tail2(b):
            j = b % 2
            Ste.wait(dg, ffnw_ld)
            bank[6].acquire(Ste)
            bsr_done = 0
            for c in range(DC):
                Ste.wait(ve, x1_done[b][c])
                fn = lambda e, c_=c: e.matmul(
                    ps[6].ap()[0:32, :], vrb.ap()[:, c_ * 32:(c_ + 1) * 32],
                    ch(x1b, c_), start=(c_ == 0), stop=(c_ == DC - 1))
                if c == DC - 1:
                    bsr_done = Ste.inc(te, 1, fn)
                else:
                    Ste.emit(fn)
            Sse.wait(te, bsr_done)
            s12 = Sse.inc(se, 1, lambda e: e.activation(
                out=bsr_b.ap(), in_=ps[6].ap()[0:32, :], func=AF.Identity))
            bank[6].release(se, s12)
            for m in range(DC):
                bk = bank[2 + m % 2]
                bk.acquire(Ste)
                for c in range(DC):
                    Ste.emit(lambda e, c_=c, m_=m: e.matmul(
                        ps[2 + m_ % 2].ap(), fw_t(c_, m_), ch(x1b, c_),
                        start=(c_ == 0), stop=False))
                Ste.wait(se, s12)
                Ste.emit(lambda e, m_=m: e.matmul(
                    ps[2 + m_ % 2].ap(), ctb.ap()[:, m_ * 128:(m_ + 1) * 128],
                    bsr_b.ap(), start=False, stop=False))
                f_done = Ste.inc(te, 1, lambda e, m_=m: e.matmul(
                    ps[2 + m_ % 2].ap(), eyeb.ap()[:, 128:256], ch(x1b, m_),
                    start=False, stop=True))
                if m == DC - 1:
                    ffn_last[b] = f_done
                Sse.wait(te, f_done)
                Sse.wait(te, ln2s_done[b])
                Sse.wait(ve, x1_done[b][m])
                Sse.emit(lambda e, m_=m: e.activation(
                    out=ch(uv, m_), in_=ps[2 + m_ % 2].ap(), func=AF.Identity,
                    bias=ffn_bv.ap()[:, m_:m_ + 1]))
                Sse.emit(lambda e, m_=m: e.activation(
                    out=ch(sqt, m_), in_=ps[2 + m_ % 2].ap(), func=AF.Square,
                    bias=ffn_bv.ap()[:, m_:m_ + 1]))
                vev[b][m] = Sse.inc(se, 1, lambda e, m_=m: e.activation(
                    out=ch(ub, m_), in_=ps[2 + m_ % 2].ap(), func=AF.Identity,
                    bias=ffn_bv.ap()[:, m_:m_ + 1]))
                bk.release(se, vev[b][m])
            bank[6].acquire(Ste)
            for c in range(DC):
                Ste.wait(se, vev[b][c])
                fn = lambda e, c_=c: e.matmul(
                    ps[6].ap(), onesn.ap(), ch(ub, c_),
                    start=(c_ == 0), stop=(c_ == DC - 1))
                if c == DC - 1:
                    mu3 = Ste.inc(te, 1, fn)
                else:
                    Ste.emit(fn)
            bank[7].acquire(Ste)
            for c in range(DC):
                fn = lambda e, c_=c: e.matmul(
                    ps[7].ap(), onesp.ap(), ch(sqt, c_),
                    start=(c_ == 0), stop=(c_ == DC - 1))
                if c == DC - 1:
                    ln3s_done[b] = Ste.inc(te, 1, fn)
                else:
                    Ste.emit(fn)
            Sse.wait(te, mu3)
            Sse.wait(te, musq2_mm[b])
            Sse.wait(ve, x1_done[b][DC - 1])
            mucopy3 = Sse.inc(se, 1, lambda e: e.activation(
                out=mu_sb.ap(), in_=ps[6].ap(), func=AF.Identity))
            bank[6].release(se, mucopy3)
            sm3 = Sse.inc(se, 1, lambda e: e.activation(
                out=sqmu1.ap(), in_=mu_sb.ap()[0:1, :], func=AF.Square))
            Ste.wait(se, sm3)
            musq3_mm[b] = Ste.inc(te, 1, lambda e: e.matmul(
                ps[7].ap(), negones1.ap(), sqmu1.ap(), start=False, stop=True,
                skip_group_check=True))
            Sse.wait(te, musq3_mm[b])
            Sse.emit(lambda e: e.activation(out=lnv.ap(), in_=ps[7].ap(),
                                            func=AF.Ln, bias=epsT.ap()))
            Sse.wait(ve, x1_done[b][DC - 1])
            rs3_done[b] = Sse.inc(se, 1, lambda e: e.activation(
                out=rs_b.ap(), in_=lnv.ap(), func=AF.Exp, scale=-0.5))
            Sve.wait(se, rs3_done[b])
            for c in range(DC):
                Sve.emit(lambda e, c_=c: e.tensor_tensor(
                    ps[7].ap(), ch(uv, c_), mu_sb.ap(), ALU.add))
                fn = lambda e, c_=c: e.tensor_tensor(
                    ch(uv, c_), ps[7].ap(), rs_b.ap(), ALU.mult)
                if c == DC - 1:
                    out_done[b] = Sve.inc(ve, 1, fn)
                else:
                    Sve.emit(fn)
            bank[7].release(ve, out_done[b])
            Ssy.wait(ve, out_done[b])
            t0 = b * C
            Ssy.inc(do, 16, lambda e, t0_=t0: e.dma_start(
                out=y_d.rearrange("(c p) t -> p c t", p=128)[:, :, t0_:t0_ + C],
                in_=uv.ap().rearrange("p (c t) -> p c t", c=DC)))
            if b + 2 < NCH:
                Sgp.wait(te, op_done[b][DC - 1])
                emit_x0_load(b + 2)
                emit_s1(b + 2)

        a1_done = [0] * NCH
        mu2_done = [0] * NCH
        ct_done = [0] * NCH
        dec_done = [0] * NCH
        y_done = [0] * NCH
        tanh_done = [[0] * DC for _ in range(NCH)]
        for i in range(NCH + 1):
            b = i - 1
            if i < NCH:
                stage_a_stats1(i)
            if b >= 0:
                stage_b_g(b)
            if i < NCH:
                stage_a_stats2(i)
            if b >= 0:
                stage_b_gates_nle(b)
            if i < NCH:
                stage_a_h(i, 0, 1)
                if b >= 0:
                    stage_b_gates_exp(b)
                for m in range(1, DC):
                    stage_a_h(i, m, m + 1)
                    if b >= 0:
                        emit_conv_te(b, m - 1)
                if b >= 0:
                    emit_conv_te(b, DC - 1)
            elif b >= 0:
                stage_b_gates_exp(b)
                for c in range(DC):
                    emit_conv_te(b, c)
            if b >= 0:
                stage_b_op(b)
                stage_b_tail1(b)
            if i < NCH:
                stage_a_h(i, DC, 16)
            if b >= 0:
                stage_b_tail2(b)

        final = do.n
        Ssy.emit(lambda e, v=final: e.wait_ge(do.h, v))

        block.sync(Ssy.replay)
        block.gpsimd(Sgp.replay)
        block.scalar(Sse.replay)
        block.vector(Sve.replay)
        block.tensor(Ste.replay)

    return nc


_NC = None


def _conv_diag(conv_w):
    out = np.zeros((128, 3 * DC * 128), dtype=np.float32)
    idx = np.arange(128)
    for c in range(DC):
        for k in range(3):
            out[idx, (c * 3 + k) * 128 + idx] = conv_w[c * 128 + idx, 0, k]
    return out


def kernel(L0_attn_x, L0_attn_norm_w, L0_attn_norm_b, L0_attn_in_w, L0_attn_in_b,
           L0_attn_conv_w, L0_attn_conv_b, L0_attn_dt_w, L0_attn_dt_b,
           L0_attn_b_w, L0_attn_b_b, L0_attn_c_w, L0_attn_c_b, L0_attn_a_log,
           L0_attn_d, L0_attn_out_w, L0_attn_out_b, L0_ffn_W, L0_ffn_b,
           L0_ffn_V_r, L0_ffn_C):
    global _NC
    if _NC is None:
        _NC = build_program()
    nc = _NC

    f32 = np.float32
    asc = np.ascontiguousarray
    x = np.asarray(L0_attn_x, dtype=f32)

    shared = {
        "in_wT": asc(np.asarray(L0_attn_in_w, f32).T),
        "g_wT": asc(np.concatenate([np.asarray(L0_attn_dt_w, f32),
                                    np.asarray(L0_attn_b_w, f32),
                                    np.asarray(L0_attn_c_w, f32)], 0).T),
        "out_wT": asc(np.asarray(L0_attn_out_w, f32).T),
        "ffn_wT": asc(np.asarray(L0_ffn_W, f32).T),
        "v_r": asc(np.asarray(L0_ffn_V_r, f32)),
        "c_T": asc(np.asarray(L0_ffn_C, f32).T),
        "eye2": asc(np.concatenate([2.0 * np.eye(128, dtype=f32),
                                    np.eye(128, dtype=f32)], 1)),
        "in_bv_hbm": asc(np.asarray(L0_attn_in_b, f32).reshape(16, 128).T),
        "norm_wv_hbm": asc(np.asarray(L0_attn_norm_w, f32).reshape(DC, 128).T),
        "norm_bv_hbm": asc(np.asarray(L0_attn_norm_b, f32).reshape(DC, 128).T),
        "out_bv_hbm": asc(np.asarray(L0_attn_out_b, f32).reshape(DC, 128).T),
        "ffn_bv_hbm": asc(np.asarray(L0_ffn_b, f32).reshape(DC, 128).T),
        "g_bv_hbm": asc(np.stack([np.asarray(L0_attn_dt_b, f32),
                              np.asarray(L0_attn_b_b, f32),
                              np.asarray(L0_attn_c_b, f32)], 1)),
        "conv_wd_hbm": _conv_diag(np.asarray(L0_attn_conv_w, f32)),
        "conv_bv_hbm": asc(np.asarray(L0_attn_conv_b, f32).reshape(DC, 128).T),
        "d_v_hbm": asc(np.asarray(L0_attn_d, f32).reshape(DC, 128).T),
        "alog_v_hbm": asc(np.asarray(L0_attn_a_log, f32).reshape(S, 1)),
    }

    in_maps = []
    for core in range(NCORES):
        m = dict(shared)
        m["x"] = asc(x[core * BL:(core + 1) * BL].reshape(TOK, D).T)
        in_maps.append(m)

    res = run_bass_kernel_spmd(nc, in_maps, core_ids=list(range(NCORES)))
    outs = []
    for core in range(NCORES):
        yc = np.asarray(res.results[core]["y"])
        outs.append(yc.T.reshape(BL, T, D))
    return np.concatenate(outs, 0).astype(np.float32)
